# version 02 (final)
# baseline (speedup 1.0000x reference)
import numpy as np
import ml_dtypes
import concourse.bass as bass
import concourse.mybir as mybir
from concourse.bass_utils import run_bass_kernel_spmd

F32 = mybir.dt.float32
BF16 = mybir.dt.bfloat16
AF = mybir.ActivationFunctionType
ALU = mybir.AluOpType
AX = mybir.AxisListType.X

NCORES = 8
D = 1024
NCH = 8
S_FULL = 4096
S_OWN = 2048
TOK = 512
HEADS_OWN = 8
HD = 64
NEG = -1.0e30
PAIRS = [[0, 1], [2, 3], [4, 5], [6, 7]]

bf = ml_dtypes.bfloat16



def stage_w(W, pad_in=0, pad_out=0):
    W = np.asarray(W)
    di, do = W.shape
    if pad_in or pad_out:
        W = np.pad(W, ((0, pad_in), (0, pad_out)))
        di += pad_in
        do += pad_out
    assert di % 128 == 0 and do % 128 == 0, (di, do)
    nk, nm = di // 128, do // 128
    return np.ascontiguousarray(
        W.reshape(nk, 128, nm, 128).transpose(2, 0, 1, 3)).astype(bf)


def stage_col(v, pad=0):
    v = np.asarray(v, dtype=np.float32)
    if pad:
        v = np.pad(v, (0, pad))
    assert v.size % 128 == 0
    return np.ascontiguousarray(v.reshape(-1, 128).T)


def fm(x):
    T = x.shape[0]
    return np.ascontiguousarray(
        np.asarray(x, dtype=np.float32).T.reshape(NCH, 128, T))



ENGINES = ("sync", "tensor", "vector", "scalar", "gpsimd")


class Sem:
    def __init__(self, h, name):
        self.h = h
        self.name = name
        self.n = 0

    def inc(self, inst, k=1):
        inst.then_inc(self.h, k)
        self.n += k
        return self.n


class Builder:
    def __init__(self, nc, stack):
        self.nc = nc
        self.stack = stack
        self.q = {e: [] for e in ENGINES}
        self.sems = []
        self._sb_off = 0
        self._marks = []

    def sem(self, name):
        s = Sem(self.stack.enter_context(self.nc.semaphore(name)), name)
        self.sems.append(s)
        return s

    def sbuf(self, name, shape, dtype):
        t = self.stack.enter_context(self.nc.sbuf_tensor(name, shape, dtype))
        return t

    def psum(self, name, shape, dtype):
        return self.stack.enter_context(self.nc.psum_tensor(name, shape, dtype))

    def dram(self, name, shape, dtype, kind=None):
        if kind:
            return self.nc.dram_tensor(name, shape, dtype, kind=kind)
        return self.nc.dram_tensor(name, shape, dtype)

    def on(self, eng, fn):
        self.q[eng].append(fn)

    def wait(self, eng, sem, val):
        assert val <= sem.n, f"wait {sem.name}>={val} but only {sem.n} emitted"
        self.q[eng].append(lambda e: e.wait_ge(sem.h, val))

    def finish(self):
        nc = self.nc
        qs = self.q
        with nc.Block() as block:
            @block.sync
            def _(e):
                for fn in qs["sync"]:
                    fn(e)

            @block.tensor
            def _(e):
                for fn in qs["tensor"]:
                    fn(e)

            @block.vector
            def _(e):
                for fn in qs["vector"]:
                    fn(e)

            @block.scalar
            def _(e):
                for fn in qs["scalar"]:
                    fn(e)

            @block.gpsimd
            def _(e):
                for fn in qs["gpsimd"]:
                    fn(e)

    def barrier(self, sem, engines=ENGINES):
        k = len(engines)
        for eng in engines:
            def mk(eng=eng):
                def f(e):
                    if eng in ("sync",):
                        e.nop().then_inc(sem.h, 1)
                    else:
                        e.sem_inc(sem.h, 1)
                return f
            self.on(eng, mk())
        sem.n += k
        tgt = sem.n
        for eng in engines:
            self.q[eng].append(lambda e, tgt=tgt: e.wait_ge(sem.h, tgt))



def ceil_div(a, b):
    return (a + b - 1) // b


def copy_any(kind, e, dst, src):
    return e.copy(dst, src) if kind == "scalar" else e.tensor_copy(dst, src)


class Ctx:

    def __init__(self, B: Builder):
        self.B = B
        self.psf = [B.psum(f"psf{i}", [128, 512], F32) for i in range(6)]
        self.psb = B.psum("psbt", [128, 1024], BF16)
        self.psb2 = B.psum("psbt2", [128, 1024], BF16)
        self.psbs = (self.psb, self.psb2)
        self.lw = B.sem("lw")
        self.lx = B.sem("lx")
        self.mm = B.sem("mm")
        self.evv = B.sem("evv")
        self.evs = B.sem("evs")
        self.gp = B.sem("gp")
        self.cc = B.sem("cc")
        self.st = B.sem("st")
        self.bar = B.sem("bar")
        self.lt = B.sem("lt")
        self.ones_row = B.sbuf("ones_row", [1, 128], F32)
        self.ones_col = B.sbuf("ones_col", [128, 1], F32)
        self.ident = B.sbuf("ident_bf", [128, 128], BF16)

    def init_consts(self, ident_dram):
        B, lx, gp = self.B, self.lx, self.gp
        gp.n += 1
        B.on("gpsimd", lambda e: (
            e.memset(self.ones_row[:], 1.0),
            e.memset(self.ones_col[:], 1.0).then_inc(gp.h, 1)))
        lx.n += 16
        B.on("gpsimd", lambda e:
             e.dma_start(out=self.ident[:], in_=ident_dram).then_inc(lx.h, 16))

    def bar_all(self):
        self.B.barrier(self.bar)


def proj_fm(C: Ctx, w_dram, n_m, n_k, rhs_fn, n_t, epilogue, wbuf, tok=TOK,
            banks=(0, 1, 2, 3), epi_eng=("vector", "scalar")):
    B = C.B
    lw, mm, evv, evs = C.lw, C.mm, C.evv, C.evs
    nb = len(banks)
    n_groups = n_m * n_t
    lw0, mm0 = lw.n, mm.n
    two = len(epi_eng) == 2
    evv0, evs0 = evv.n, evs.n

    lw.n += 16 * n_m
    mm.n += n_groups
    if two:
        evv.n += (n_groups + 1) // 2
        evs.n += n_groups // 2
    else:
        (evv if epi_eng[0] == "vector" else evs).n += n_groups

    def epi_sem_cnt(gi):
        if two:
            if gi % 2 == 1:
                return evv, evv0 + (gi + 1) // 2
            return evs, evs0 + gi // 2
        s = evv if epi_eng[0] == "vector" else evs
        return s, (evv0 if s is evv else evs0) + gi

    for i in range(n_m):
        if i >= 2:
            tgt = mm0 + (i - 1) * n_t
            B.on("sync", lambda e, tgt=tgt: e.wait_ge(mm.h, tgt))
        else:
            B.on("sync", lambda e, tgt=mm0: e.wait_ge(mm.h, tgt))
        b = (i % 2) * n_k * 128
        B.on("sync", lambda e, i=i, b=b: e.dma_start(
            out=wbuf[:, b:b + n_k * 128].rearrange("p (c m) -> p c m", c=n_k),
            in_=w_dram[i].rearrange("c p m -> p c m")).then_inc(lw.h, 16))

    B.on("tensor", lambda e, tgt=evv0: e.wait_ge(evv.h, tgt))
    B.on("tensor", lambda e, tgt=evs0: e.wait_ge(evs.h, tgt))
    gi = 0
    for i in range(n_m):
        B.on("tensor", lambda e, tgt=lw0 + 16 * (i + 1): e.wait_ge(lw.h, tgt))
        b = (i % 2) * n_k * 128
        for tt in range(n_t):
            gi += 1
            bank = banks[(gi - 1) % nb]
            ps = C.psf[bank]
            if gi > nb:
                s, cnt = epi_sem_cnt(gi - nb)
                B.on("tensor", lambda e, s=s, cnt=cnt: e.wait_ge(s.h, cnt))

            def fmm(e, i=i, b=b, tt=tt, ps=ps, gi=gi):
                inst = None
                for c in range(n_k):
                    inst = e.matmul(ps[:, 0:tok],
                                    wbuf[:, b + c * 128:b + (c + 1) * 128],
                                    rhs_fn(c, tt), start=(c == 0),
                                    stop=(c == n_k - 1))
                inst.then_inc(mm.h, 1)
            B.on("tensor", fmm)

    for gi in range(1, n_groups + 1):
        i, tt = (gi - 1) // n_t, (gi - 1) % n_t
        ps = C.psf[banks[(gi - 1) % nb]]
        s, cnt = epi_sem_cnt(gi)
        kind = "vector" if s is evv else "scalar"
        B.on(kind, lambda e, tgt=mm0 + gi: e.wait_ge(mm.h, tgt))
        B.on(kind, lambda e, kind=kind, ps=ps, i=i, tt=tt, s=s:
             epilogue(kind, e, ps[:, 0:tok], i, tt).then_inc(s.h, 1))


def ln_fm(C: Ctx, n_t, src_fn, dst_fn, eps, sq0, sq1, rows, bcbuf,
          gcol=None, bcol=None, dvar=D, tok=TOK, nch=NCH):
    B = C.B
    mm, evv, evs = C.mm, C.evv, C.evs
    pA, pB, pC = C.psf[4], C.psf[5], C.psf[3]
    sq = (sq0, sq1)
    B.on("tensor", lambda e, t=evv.n: e.wait_ge(evv.h, t))
    B.on("tensor", lambda e, t=evs.n: e.wait_ge(evs.h, t))
    for tt in range(n_t):
        mm0 = mm.n
        evv0 = evv.n
        for c in range(nch):
            if c >= 2:
                B.on("vector", lambda e, tgt=mm0 + c - 1: e.wait_ge(mm.h, tgt))
            evv.n += 1
            B.on("vector", lambda e, c=c, tt=tt, cnt=evv.n: e.tensor_tensor(
                sq[c % 2][:, 0:tok], src_fn(c, tt), src_fn(c, tt), ALU.mult
            ).then_inc(evv.h, 1))
            mm.n += 1
            B.on("tensor", lambda e, tgt=evv.n: e.wait_ge(evv.h, tgt))

            def fmm(e, c=c, tt=tt):
                e.matmul(pA[0:1, 0:tok], C.ones_col[:], src_fn(c, tt),
                         start=(c == 0), stop=(c == nch - 1))
                e.matmul(pB[0:1, 0:tok], C.ones_col[:], sq[c % 2][:, 0:tok],
                         start=(c == 0), stop=(c == nch - 1)).then_inc(mm.h, 1)
            B.on("tensor", fmm)

        r_mean = rows[0:1, 0:tok]
        r_var = rows[0:1, tok:2 * tok]
        r_tmp = rows[0:1, 2 * tok:3 * tok]
        r_rstd = rows[0:1, 3 * tok:4 * tok]
        B.on("vector", lambda e, tgt=mm.n: e.wait_ge(mm.h, tgt))
        evv.n += 1
        B.on("vector", lambda e: (
            e.tensor_scalar(r_mean, pA[0:1, 0:tok], 1.0 / dvar, None, ALU.mult),
            e.tensor_scalar(r_var, pB[0:1, 0:tok], 1.0 / dvar, None, ALU.mult),
            e.tensor_tensor(r_tmp, r_mean, r_mean, ALU.mult),
            e.tensor_tensor(r_var, r_var, r_tmp, ALU.subtract),
            e.tensor_scalar(r_var, r_var, eps, None, ALU.add),
            e.tensor_copy(r_rstd, r_var).then_inc(evv.h, 1)))
        B.on("scalar", lambda e, tgt=evv.n: e.wait_ge(evv.h, tgt))
        evs.n += 1
        B.on("scalar", lambda e: e.activation(
            r_rstd, r_rstd, AF.Sqrt).then_inc(evs.h, 1))
        B.on("vector", lambda e, tgt=evs.n: e.wait_ge(evs.h, tgt))
        evv.n += 1
        B.on("vector", lambda e: (
            e.reciprocal(r_rstd, r_rstd),
            e.tensor_tensor(r_mean, r_mean, r_rstd, ALU.mult).then_inc(evv.h, 1)))
        B.on("tensor", lambda e, tgt=evv.n: e.wait_ge(evv.h, tgt))
        mm.n += 1
        B.on("tensor", lambda e: (
            e.matmul(pC[:, 0:tok], C.ones_row[:], r_rstd, start=True, stop=True),
            e.matmul(pA[:, 0:tok], C.ones_row[:], r_mean,
                     start=True, stop=True).then_inc(mm.h, 1)))
        B.on("vector", lambda e, tgt=mm.n: e.wait_ge(mm.h, tgt))
        evv.n += 1
        B.on("vector", lambda e: (
            e.tensor_copy(bcbuf[:, 0:tok], pC[:, 0:tok]),
            e.tensor_copy(bcbuf[:, tok:2 * tok], pA[:, 0:tok]).then_inc(evv.h, 1)))
        for c in range(nch):
            evv.n += 1

            def fnorm(e, c=c, tt=tt):
                e.tensor_tensor(sq0[:, 0:tok], src_fn(c, tt), bcbuf[:, 0:tok],
                                ALU.mult)
                if gcol is None:
                    inst = e.tensor_tensor(dst_fn(c, tt), sq0[:, 0:tok],
                                           bcbuf[:, tok:2 * tok], ALU.subtract)
                else:
                    e.tensor_tensor(sq0[:, 0:tok], sq0[:, 0:tok],
                                    bcbuf[:, tok:2 * tok], ALU.subtract)
                    inst = e.tensor_scalar(dst_fn(c, tt), sq0[:, 0:tok], gcol(c),
                                           bcol(c), ALU.mult, ALU.add)
                inst.then_inc(evv.h, 1)
            B.on("vector", fnorm)


class Arena:
    def __init__(self, B, name, cols, dtype):
        self.t = B.sbuf(name, [128, cols], dtype)
        self.top = 0
        self.cap = cols

    def alloc(self, n):
        o = self.top
        self.top = o + n
        assert self.top <= self.cap, f"arena {self.t} over: {self.top}>{self.cap}"
        return self.t[:, o:o + n]

    def mark(self):
        return self.top

    def reset(self, m):
        self.top = m


class AttnSems:
    def __init__(self, B):
        self.asm = B.sem("a_sm")
        self.asv = B.sem("a_sv")
        self.amax = B.sem("a_max")
        self.aexp = B.sem("a_exp")
        self.anrm = B.sem("a_nrm")
        self.amt = B.sem("a_mt")
        self.aevt = B.sem("a_evt")
        self.amo = B.sem("a_mo")
        self.aeo = B.sem("a_eo")
        self.it = 0
        self.win = 0
        self.tb = 0
        self.win_hist = []
        self.tb_hist = []


def attn_run(C: Ctx, AS: AttnSems, iters, bufs):
    B = C.B
    S_sb, P_bf, PT_sb, stat = bufs["S_sb"], bufs["P_bf"], bufs["PT_sb"], bufs["stat"]
    n = len(iters)
    B.on("tensor", lambda e, t=C.evv.n: e.wait_ge(C.evv.h, t))
    B.on("tensor", lambda e, t=C.evs.n: e.wait_ge(C.evs.h, t))
    for eng, sem, val in bufs.get("pre", []):
        B.on(eng, lambda e, sem=sem, val=val: e.wait_ge(sem.h, val))

    win_of_iter = AS.win_hist
    tb_of_iter = AS.tb_hist
    serialize = bufs.get("serialize", False)
    for it in iters:
        i = AS.it
        if serialize and i >= 1:
            B.on("tensor", lambda e, t=i: e.wait_ge(AS.aeo.h, t))
        wlens = it["wlens"]
        L = sum(wlens)
        nj = L // 128
        par = i % 2
        sps = C.psf[par]
        ops = C.psf[2 + par]
        win_of_iter.append((AS.win, len(wlens)))

        if i >= 2:
            sw, nw = win_of_iter[i - 2]
            B.on("tensor", lambda e, t=sw + nw: e.wait_ge(AS.asv.h, t))
        woff = 0
        for w, wl in enumerate(wlens):
            if w >= 1:
                B.on("tensor", lambda e, t=AS.asv.n: e.wait_ge(AS.asv.h, t))
            AS.win += 1
            AS.asm.n += 1

            def fsmm(e, it=it, w=w, wl=wl, sps=sps):
                jmm = it.get("jmm")
                if jmm is not None and jmm[w] is not None:
                    e.matmul(sps[:, 0:wl], it["q"], it["wins"][w][0],
                             start=True, stop=False)
                    lhs_j, rhs_j = jmm[w]
                    e.matmul(sps[:, 0:wl], lhs_j, rhs_j,
                             start=False, stop=True).then_inc(AS.asm.h, 1)
                else:
                    e.matmul(sps[:, 0:wl], it["q"], it["wins"][w][0],
                             start=True, stop=True).then_inc(AS.asm.h, 1)
            B.on("tensor", fsmm)
            B.on("vector", lambda e, t=AS.asm.n: e.wait_ge(AS.asm.h, t))
            AS.asv.n += 1

            def fev(e, it=it, w=w, wl=wl, woff=woff, par=par, sps=sps):
                dst = S_sb[par][:, woff:woff + wl]
                jb = it.get("jb")
                if jb is not None and jb[w] is not None:
                    inst = e.scalar_tensor_tensor(
                        dst, sps[:, 0:wl], it["scale"], jb[w], ALU.mult, ALU.add)
                else:
                    inst = e.tensor_scalar(dst, sps[:, 0:wl], it["scale"],
                                           None, ALU.mult)
                for kind, m in it["wins"][w][1]:
                    inst = e.tensor_tensor(dst, dst, m, ALU.add)
                inst.then_inc(AS.asv.h, 1)
            B.on("vector", fev)
            woff += wl

        AS.amax.n += 1

        def fmax(e, it=it, par=par, L=L):
            e.reduce_max(stat[:, par * 4:par * 4 + 1], S_sb[par][:, 0:L],
                         axis=AX, negate=True).then_inc(AS.amax.h, 1)
        B.on("vector", fmax)

        B.on("scalar", lambda e, t=AS.amax.n: e.wait_ge(AS.amax.h, t))
        if i >= 2:
            B.on("scalar", lambda e, t=tb_of_iter[i - 2][0] + tb_of_iter[i - 2][1]:
                 e.wait_ge(AS.amt.h, t))
        AS.aexp.n += 1
        B.on("scalar", lambda e, it=it, par=par, L=L: e.activation(
            P_bf[par][:, 0:L], S_sb[par][:, 0:L], AF.Exp,
            bias=stat[:, par * 4:par * 4 + 1], scale=1.0,
            accum_out=stat[:, par * 4 + 1:par * 4 + 2]).then_inc(AS.aexp.h, 1))

        B.on("vector", lambda e, t=AS.aexp.n: e.wait_ge(AS.aexp.h, t))
        AS.anrm.n += 1
        B.on("vector", lambda e, par=par: e.reciprocal(
            stat[:, par * 4 + 2:par * 4 + 3],
            stat[:, par * 4 + 1:par * 4 + 2]).then_inc(AS.anrm.h, 1))
        B.on("gpsimd", lambda e, t=AS.anrm.n: e.wait_ge(AS.anrm.h, t))
        AS.anrm.n += 1
        B.on("gpsimd", lambda e, par=par, L=L: e.tensor_scalar(
            P_bf[par][:, 0:L], P_bf[par][:, 0:L],
            stat[:, par * 4 + 2:par * 4 + 3], None,
            ALU.mult).then_inc(AS.anrm.h, 1))

        B.on("tensor", lambda e, t=AS.anrm.n: e.wait_ge(AS.anrm.h, t))
        tb0 = AS.tb
        nb_batches = ceil_div(nj, 2)
        tb_of_iter.append((tb0, nb_batches))
        if i >= 2:
            B.on("tensor", lambda e, t=i - 1: e.wait_ge(AS.aeo.h, t))
        if i >= 2:
            B.on("scalar", lambda e, t=i - 1: e.wait_ge(AS.amo.h, t))
        for bb in range(nb_batches):
            j0 = bb * 2
            js = [j for j in (j0, j0 + 1) if j < nj]
            half = AS.tb % 2
            if AS.tb >= 2:
                B.on("tensor", lambda e, t=AS.tb - 1: e.wait_ge(AS.aevt.h, t))
            AS.tb += 1
            AS.amt.n += 1

            def ftr(e, js=js, par=par, half=half):
                inst = None
                for idx, j in enumerate(js):
                    inst = e.transpose(
                        C.psbs[half][:, idx * 128:idx * 128 + 128],
                        P_bf[par][:, j * 128:(j + 1) * 128], C.ident[:])
                inst.then_inc(AS.amt.h, 1)
            B.on("tensor", ftr)
            B.on("scalar", lambda e, t=AS.amt.n: e.wait_ge(AS.amt.h, t))
            AS.aevt.n += 1
            B.on("scalar", lambda e, js=js, par=par, half=half, j0=j0:
                 e.copy(PT_sb[par][:, j0 * 128:j0 * 128 + len(js) * 128],
                        C.psbs[half][:, 0:len(js) * 128]
                        ).then_inc(AS.aevt.h, 1))
            B.on("tensor", lambda e, t=AS.aevt.n: e.wait_ge(AS.aevt.h, t))
            last_batch = bb == nb_batches - 1
            AS.amo.n += 1 if last_batch else 0

            def fav(e, js=js, par=par, it=it, j0=j0, first=(bb == 0),
                    last=last_batch, ops=ops):
                inst = None
                for idx, j in enumerate(js):
                    inst = e.matmul(ops[0:64, 0:128], it["v"][j],
                                    PT_sb[par][:, j * 128:(j + 1) * 128],
                                    start=(first and idx == 0),
                                    stop=(last and idx == len(js) - 1))
                if last:
                    inst.then_inc(AS.amo.h, 1)
            B.on("tensor", fav)

        B.on("vector", lambda e, t=AS.amo.n: e.wait_ge(AS.amo.h, t))
        AS.aeo.n += 1
        B.on("vector", lambda e, it=it, ops=ops: e.tensor_copy(
            it["out"], ops[0:64, 0:128]).then_inc(AS.aeo.h, 1))
        AS.it += 1



def proj_tok(C: Ctx, w_sb, n_k, n_out, lhs_fn, n_j, epilogue,
             banks=(0, 1, 2, 3), epi_eng=("vector", "scalar")):
    B = C.B
    mm, evv, evs = C.mm, C.evv, C.evs
    nb = len(banks)
    mm0, evv0, evs0 = mm.n, evv.n, evs.n
    two = len(epi_eng) == 2
    mm.n += n_j
    if two:
        evv.n += (n_j + 1) // 2
        evs.n += n_j // 2
    else:
        (evv if epi_eng[0] == "vector" else evs).n += n_j

    def epi_sem_cnt(gi):
        if two:
            if gi % 2 == 1:
                return evv, evv0 + (gi + 1) // 2
            return evs, evs0 + gi // 2
        s = evv if epi_eng[0] == "vector" else evs
        return s, (evv0 if s is evv else evs0) + gi

    B.on("tensor", lambda e, t=evv0: e.wait_ge(evv.h, t))
    B.on("tensor", lambda e, t=evs0: e.wait_ge(evs.h, t))
    for j in range(n_j):
        gi = j + 1
        ps = C.psf[banks[j % nb]]
        if gi > nb:
            s, cnt = epi_sem_cnt(gi - nb)
            B.on("tensor", lambda e, s=s, cnt=cnt: e.wait_ge(s.h, cnt))

        def fmm(e, j=j, ps=ps):
            inst = None
            for c in range(n_k):
                inst = e.matmul(ps[:, 0:n_out], lhs_fn(c, j),
                                w_sb[:, c * n_out:(c + 1) * n_out],
                                start=(c == 0), stop=(c == n_k - 1))
            inst.then_inc(mm.h, 1)
        B.on("tensor", fmm)
    for j in range(n_j):
        gi = j + 1
        ps = C.psf[banks[j % nb]]
        s, cnt = epi_sem_cnt(gi)
        kind = "vector" if s is evv else "scalar"
        B.on(kind, lambda e, tgt=mm0 + gi: e.wait_ge(mm.h, tgt))
        B.on(kind, lambda e, kind=kind, ps=ps, j=j, s=s:
             epilogue(kind, e, ps[:, 0:n_out], j).then_inc(s.h, 1))


def load_wsb(C, dst_sb, w_dram, n_k, n_out, queue="sync"):
    B = C.B
    C.lw.n += 16
    B.on(queue, lambda e: e.dma_start(
        out=dst_sb.rearrange("p (k m) -> p k m", k=n_k),
        in_=w_dram.rearrange("k p m -> p k m")).then_inc(C.lw.h, 16))




from contextlib import ExitStack


def dump_sb(B, C, name, ap, shape, dtype=F32):
    d = B.dram(name, shape, dtype, kind="ExternalOutput").ap()
    C.st.n += 16
    B.on("gpsimd", lambda e: e.dma_start(out=d, in_=ap).then_inc(C.st.h, 16))


def build_A(stop="all", dbg=False, drc=True):
    stack = ExitStack()
    nc = bass.Bass("TRN2", target_bir_lowering=False, detect_race_conditions=drc)
    B = Builder(nc, stack)
    C = Ctx(B)
    AS = AttnSems(B)
    ap = lambda t: t.ap()

    ident_d = ap(B.dram("ident", [128, 128], BF16, "ExternalInput"))
    xo_d = ap(B.dram("xo", [NCH, 128, S_OWN], F32, "ExternalInput"))
    xoh_d = ap(B.dram("xoh", [NCH, 128, 512], F32, "ExternalInput"))
    xq_d = ap(B.dram("xq", [NCH, 128, S_FULL], F32, "ExternalInput"))
    cdiag_d = ap(B.dram("cdiag", [128, 4 * 512], BF16, "ExternalInput"))
    cols_d = ap(B.dram("cols", [128, 160], F32, "ExternalInput"))
    rb_d = ap(B.dram("rbrow", [1, 8], F32, "ExternalInput"))
    ebrows_d = ap(B.dram("ebrows", [8, 1024], F32, "ExternalInput"))
    w_names = dict(
        wq=[4, 8], wk=[4, 8], wo=[8, 4],
        cw1=[4, 8], cw2=[8, 4], pww=[8, 8], oww=[8, 8],
        fdw=[2, 8], fuw=[8, 2],
    )
    W = {k: ap(B.dram(k, [m, kk, 128, 128], BF16, "ExternalInput"))
         for k, (m, kk) in w_names.items()}
    ewr_d = ap(B.dram("ewr", [8, 8, 128, 1024], BF16, "ExternalInput"))
    wvr_d = ap(B.dram("wvr", [8, 128, 512], BF16, "ExternalInput"))
    rwr_d = ap(B.dram("rwr", [8, 128, 8], F32, "ExternalInput"))

    x1o_d = ap(B.dram("x1o", [NCH, 128, S_OWN], F32, "ExternalOutput"))

    ohp_t = B.dram("ohp", [S_FULL, D], F32)
    xnod_t = B.dram("xnod", [NCH, 128, 2 + S_OWN], BF16)
    xpre_t = B.dram("xpre", [NCH, 128, S_OWN], F32)
    rso_t = B.dram("rso", [S_OWN, D], F32)
    rsob_t = B.dram("rsob", [S_OWN, D], BF16)

    AF32 = Arena(B, "af32", 25152, F32)
    ABF = Arena(B, "abf", 47872, BF16)
    xres = AF32.alloc(NCH * S_OWN)
    cols = B.sbuf("colsb", [128, 160], F32)
    rbrow = B.sbuf("rbrow_sb", [1, 8], F32)
    cdiag = B.sbuf("cdiag_sb", [128, 4 * 512], BF16)
    ebrow1 = B.sbuf("ebrow1", [1, 1024], F32)
    hmask = B.sbuf("hmask_sb", [128, 2], BF16)
    xoh = B.sbuf("xoh_sb", [128, NCH * 2], F32)

    COLS = {}
    off = 0
    for nm, n in [("cb1", 4), ("cb2", 8), ("dwb", 8), ("pwb", 8),
                  ("owb", 8), ("nw", 8), ("fdb", 2), ("fub", 8),
                  ("dwk", 24), ("eb", 64), ("hm", 1), ("zz", 1)]:
        COLS[nm] = off
        off += n
    assert off <= 160

    def col(nm, i=0):
        return cols[:, COLS[nm] + i:COLS[nm] + i + 1]

    C.init_consts(ident_d)
    lx, mm, evv, evs, gp, st, cc, lw = (C.lx, C.mm, C.evv, C.evs, C.gp, C.st,
                                        C.cc, C.lw)

    xresc = lambda c: xres[:, c * S_OWN:(c + 1) * S_OWN]
    for c in range(NCH):
        lx.n += 16
        B.on("gpsimd", lambda e, c=c: e.dma_start(
            out=xresc(c), in_=xo_d[c]).then_inc(lx.h, 16))
    lx.n += 16 * 5
    B.on("gpsimd", lambda e: e.dma_start(out=cols[:], in_=cols_d).then_inc(lx.h, 16))
    B.on("gpsimd", lambda e: e.dma_start(out=rbrow[:], in_=rb_d).then_inc(lx.h, 16))
    B.on("gpsimd", lambda e: e.dma_start(out=cdiag[:], in_=cdiag_d).then_inc(lx.h, 16))
    B.on("gpsimd", lambda e: e.sem_inc(lx.h, 32))
    ld_base = lx.n
    B.on("vector", lambda e, t=ld_base: e.wait_ge(lx.h, t))
    B.on("scalar", lambda e, t=ld_base: e.wait_ge(lx.h, t))
    B.on("tensor", lambda e, t=ld_base: e.wait_ge(lx.h, t))


    sq0 = AF32.alloc(512)
    sq1 = AF32.alloc(512)
    bcbuf = AF32.alloc(1024)
    rows = AF32.alloc(2048)
    stat = AF32.alloc(16)
    cscr = AF32.alloc(512)
    Kini = ABF.alloc(4 * 1024)
    Vini = ABF.alloc(8 * 512)
    compb = ABF.alloc(NCH * 256)
    Kcr = ABF.alloc(4 * 256)
    Vcr = ABF.alloc(2 * 512)
    wbuf = ABF.alloc(2 * 8 * 128)
    Qkeep = ABF.alloc(4 * 1024)
    Qroll = [ABF.alloc(4 * 512) for _ in range(2)]
    mb_attn = ABF.mark()
    mf_attn = AF32.mark()

    Kic = lambda c: Kini[:, c * 1024:(c + 1) * 1024]
    Vic = lambda tc: Vini[:, tc * 512:(tc + 1) * 512]
    compc = lambda c: compb[:, c * 256:(c + 1) * 256]
    Kcc = lambda c: Kcr[:, c * 256:(c + 1) * 256]
    Vcc = lambda tc: Vcr[:, tc * 512:(tc + 1) * 512]
    Qkc = lambda c: Qkeep[:, c * 1024:(c + 1) * 1024]

    c1b = ABF.alloc(4 * 1024)
    wvr_sb = ABF.alloc(8 * 512)
    xnq = [ABF.alloc(NCH * 512) for _ in range(2)]
    xqb_one = AF32.alloc(NCH * 512)
    xqb = [xqb_one, xqb_one]
    c1c = lambda c: c1b[:, c * 1024:(c + 1) * 1024]

    load_wsb(C, wvr_sb, wvr_d, 8, 512)

    def stream_tile(tt, qdst):
        B.on("tensor", lambda e, t=AS.asv.n: e.wait_ge(AS.asv.h, t))
        B.on("tensor", lambda e, t=AS.aeo.n: e.wait_ge(AS.aeo.h, t))
        B.on("tensor", lambda e, t=evs.n: e.wait_ge(evs.h, t))
        buf = tt % 2
        if tt >= 1:
            B.on("gpsimd", lambda e, t=evv.n: e.wait_ge(evv.h, t))
            B.on("gpsimd", lambda e, t=evs.n: e.wait_ge(evs.h, t))
            B.on("gpsimd", lambda e, t=mm.n: e.wait_ge(mm.h, t))
        for c in range(NCH):
            lx.n += 16
            B.on("gpsimd", lambda e, c=c, tt=tt, buf=buf: e.dma_start(
                out=xqb[buf][:, c * 512:(c + 1) * 512],
                in_=xq_d[c, :, tt * 512:(tt + 1) * 512]).then_inc(lx.h, 16))
        B.on("vector", lambda e, t=lx.n: e.wait_ge(lx.h, t))
        B.on("tensor", lambda e, t=lx.n: e.wait_ge(lx.h, t))
        ln_fm(C, 1,
              lambda c, _t, buf=buf: xqb[buf][:, c * 512:(c + 1) * 512],
              lambda c, _t, buf=buf: xnq[buf][:, c * 512:(c + 1) * 512],
              1e-5, sq0, sq1, rows, bcbuf)
        B.on("tensor", lambda e, t=evv.n: e.wait_ge(evv.h, t))
        rhs = lambda c, _t, buf=buf: xnq[buf][:, c * 512:(c + 1) * 512]
        proj_fm(C, W["wq"], 4, 8, rhs, 1,
                (lambda kind, e, ps, m, _tt: copy_any(kind, e, qdst(m), ps)), wbuf)
        return rhs

    for tt in range(2):
        rhs = stream_tile(tt, lambda m, tt=tt: Qkc(m)[:, tt * 512:(tt + 1) * 512])
        proj_fm(C, W["wk"], 4, 8, rhs, 1,
                (lambda kind, e, ps, m, _tt, tt=tt: copy_any(
                    kind, e, Kic(m)[:, tt * 512:(tt + 1) * 512], ps)), wbuf)
        def c1_epi(kind, e, ps, m, _tt, tt=tt):
            dst = c1c(m)[:, tt * 512:(tt + 1) * 512]
            if kind == "scalar":
                return e.activation(dst, ps, AF.Relu, bias=col("cb1", m))
            return e.tensor_scalar(dst, ps, col("cb1", m), 0.0, ALU.add, ALU.max)
        proj_fm(C, W["cw1"], 4, 8, rhs, 1, c1_epi, wbuf)
        B.on("tensor", lambda e, t=lw.n: e.wait_ge(lw.h, t))
        proj_tok(C, wvr_sb, 8, 512,
                 (lambda c, j, buf=tt % 2: xnq[buf][:, c * 512 + j * 128:
                                                    c * 512 + (j + 1) * 128]),
                 4,
                 (lambda kind, e, ps, j, tt=tt: copy_any(
                     kind, e, Vic(tt * 4 + j), ps)))

    def comp_epi(kind, e, ps, m, tt):
        e.tensor_copy(cscr, ps)
        a = cscr.rearrange("p (t f) -> p t f", f=4)
        e.tensor_tensor(sq0[:, 0:128], a[:, :, 0], a[:, :, 1], ALU.add)
        e.tensor_tensor(sq1[:, 0:128], a[:, :, 2], a[:, :, 3], ALU.add)
        e.tensor_tensor(sq0[:, 0:128], sq0[:, 0:128], sq1[:, 0:128], ALU.add)
        return e.tensor_scalar(compc(m)[:, tt * 128:(tt + 1) * 128],
                               sq0[:, 0:128], 0.25, col("cb2", m),
                               ALU.mult, ALU.add)

    proj_fm(C, W["cw2"], 8, 4,
            (lambda c, tt: c1c(c)[:, tt * 512:(tt + 1) * 512]),
            2, comp_epi, wbuf, epi_eng=("vector",))
    B.on("tensor", lambda e, t=evv.n: e.wait_ge(evv.h, t))
    proj_fm(C, W["wk"], 4, 8, (lambda c, _tt: compc(c)), 1,
            (lambda kind, e, ps, m, _tt: copy_any(kind, e, Kcc(m), ps)),
            wbuf, tok=256)
    proj_tok(C, wvr_sb, 8, 512,
             (lambda c, j: compc(c)[:, j * 128:(j + 1) * 128]), 2,
             (lambda kind, e, ps, j: copy_any(kind, e, Vcc(j), ps)))

    if stop == "proj":
        C.bar_all()
        if dbg:
            dump_sb(B, C, "d_q", Qkeep, [128, 4 * 1024], BF16)
            dump_sb(B, C, "d_kini", Kini, [128, 4 * 1024], BF16)
            dump_sb(B, C, "d_vini", Vini, [128, 8 * 512], BF16)
            dump_sb(B, C, "d_comp", compb, [128, 8 * 256], BF16)
            dump_sb(B, C, "d_kcr", Kcr, [128, 4 * 256], BF16)
            dump_sb(B, C, "d_vcr", Vcr, [128, 2 * 512], BF16)
        finish_program(B, C)
        return nc, stack

    P_bf = [c1b[:, 0:1024], c1b[:, 1024:2048]]
    PT_sb = [c1b[:, 2048:3072], c1b[:, 3072:4096]]
    wop = wvr_sb[:, 0:512]
    out_tile = [ABF.alloc(4 * 512) for _ in range(2)]
    rowstage = ABF.alloc(4 * 1024)
    S_sb = [xqb_one[:, 0:1024], xqb_one[:, 1024:2048]]
    abufs = dict(S_sb=S_sb, P_bf=P_bf, PT_sb=PT_sb, stat=stat)

    def hQ(src, h, qt128):
        return src[(h % 2) * 64:(h % 2) * 64 + 64,
                   (h // 2) * 0:0] if False else None

    def mk_iters_causal(qg, qsrc):
        iters = []
        for qti in range(4):
            qt = qg * 4 + qti
            nwin = qt // 4 + 1
            for h in range(HEADS_OWN):
                ch, base = h // 2, (h % 2) * 64
                wins = []
                wlens = []
                for w in range(nwin):
                    diag = (w == qt // 4)
                    masks = []
                    if diag:
                        masks.append(("mask", cdiag[:, (qt % 4) * 512:
                                                    (qt % 4 + 1) * 512]))
                    wins.append((Kic(ch)[base:base + 64, w * 512:(w + 1) * 512],
                                 masks))
                    wlens.append(512)
                it = dict(
                    q=qsrc(ch)[base:base + 64, qt * 128:(qt + 1) * 128],
                    wins=wins, wlens=wlens, scale=0.125,
                    v=[Vic(j)[:, h * 64:(h + 1) * 64] for j in range(nwin * 4)],
                    out=out_tile[qg % 2][base:base + 64,
                                         (h // 2) * 512 + qti * 128:
                                         (h // 2) * 512 + (qti + 1) * 128],
                )
                iters.append(it)
        return iters

    def mk_iters_cross(qg, par):
        iters = []
        for qti in range(4):
            for h in range(HEADS_OWN):
                ch, base = h // 2, (h % 2) * 64
                it = dict(
                    q=Qroll[qg % 2][ch * 512 + base if False else 0:0] if False
                    else Qroll[qg % 2][base:base + 64,
                                       ch * 512 + qti * 128:
                                       ch * 512 + (qti + 1) * 128],
                    wins=[(Kcc(ch)[base:base + 64, 0:256], [])],
                    wlens=[256], scale=0.125,
                    v=[Vcc(j)[:, h * 64:(h + 1) * 64] for j in range(2)],
                    out=out_tile[par][base:base + 64,
                                      (h // 2) * 512 + qti * 128:
                                      (h // 2) * 512 + (qti + 1) * 128],
                )
                iters.append(it)
        return iters

    def wo_step(qg, par):
        mm0 = mm.n
        lw0 = lw.n
        st0 = st.n
        B.on("tensor", lambda e, t=AS.aeo.n: e.wait_ge(AS.aeo.h, t))
        B.on("tensor", lambda e, t=evv.n: e.wait_ge(evv.h, t))
        B.on("tensor", lambda e, t=evs.n: e.wait_ge(evs.h, t))
        for m in range(8):
            if m >= 2:
                B.on("sync", lambda e, t=mm0 + m - 1: e.wait_ge(mm.h, t))
            else:
                B.on("sync", lambda e, t=mm0: e.wait_ge(mm.h, t))
            b = (m % 2) * 4 * 128
            lw.n += 16
            B.on("sync", lambda e, m=m, b=b: e.dma_start(
                out=wbuf[:, b:b + 4 * 128].rearrange("p (c q) -> p c q", c=4),
                in_=W["wo"][m].rearrange("c p q -> p c q")).then_inc(lw.h, 16))
        B.on("scalar", lambda e, t=st0: e.wait_ge(st.h, t))
        for m in range(8):
            bank = C.psf[4 + m % 2]
            B.on("tensor", lambda e, t=lw0 + 16 * (m + 1): e.wait_ge(lw.h, t))
            if m >= 2:
                B.on("tensor", lambda e, t=evs.n - 1: e.wait_ge(evs.h, t))
            b = (m % 2) * 4 * 128
            mm.n += 1

            def fwo(e, m=m, b=b, bank=bank, par=par):
                inst = None
                for c in range(4):
                    inst = e.matmul(bank[:, 0:512],
                                    wbuf[:, b + c * 128:b + (c + 1) * 128],
                                    out_tile[par][:, c * 512:(c + 1) * 512],
                                    start=(c == 0), stop=(c == 3))
                inst.then_inc(mm.h, 1)
            B.on("tensor", fwo)
            B.on("scalar", lambda e, t=mm.n: e.wait_ge(mm.h, t))
            evs.n += 1
            B.on("scalar", lambda e, bank=bank: e.copy(
                wop, bank[:, 0:512]).then_inc(evs.h, 1))
            B.on("tensor", lambda e, t=evs.n: e.wait_ge(evs.h, t))
            for bb in range(2):
                half = AS.tb % 2
                if AS.tb >= 2:
                    B.on("tensor", lambda e, t=AS.tb - 1: e.wait_ge(AS.aevt.h, t))
                AS.tb += 1
                AS.amt.n += 1

                def ftr(e, bb=bb, half=half):
                    e.transpose(C.psbs[half][:, 0:128],
                                wop[:, bb * 256:bb * 256 + 128], C.ident[:])
                    e.transpose(C.psbs[half][:, 128:256],
                                wop[:, bb * 256 + 128:bb * 256 + 256],
                                C.ident[:]).then_inc(AS.amt.h, 1)
                B.on("tensor", ftr)
                B.on("scalar", lambda e, t=AS.amt.n: e.wait_ge(AS.amt.h, t))
                AS.aevt.n += 1

                def fevt(e, bb=bb, half=half, m=m):
                    j0 = bb * 2
                    e.copy(rowstage[:, j0 * 1024 + m * 128:j0 * 1024 + m * 128 + 128],
                           C.psbs[half][:, 0:128])
                    e.copy(rowstage[:, (j0 + 1) * 1024 + m * 128:
                                    (j0 + 1) * 1024 + m * 128 + 128],
                           C.psbs[half][:, 128:256]).then_inc(AS.aevt.h, 1)
                B.on("scalar", fevt)
        B.on("gpsimd", lambda e, t=AS.aevt.n: e.wait_ge(AS.aevt.h, t))
        for j in range(4):
            st.n += 16
            B.on("gpsimd", lambda e, j=j, qg=qg: e.dma_start(
                out=ohp_t.ap()[qg * 512 + j * 128:qg * 512 + (j + 1) * 128, :],
                in_=rowstage[:, j * 1024:(j + 1) * 1024]).then_inc(st.h, 16))

    for qg in range(2):
        ab = dict(abufs)
        ab["pre"] = [("vector", mm, mm.n)]
        attn_run(C, AS, mk_iters_causal(qg, Qkc), ab)
        wo_step(qg, qg % 2)

    for qg in range(2, 8):
        B.on("gpsimd", lambda e, t=AS.aexp.n: e.wait_ge(AS.aexp.h, t))
        stream_tile(qg, lambda m, qg=qg: Qroll[qg % 2][:, m * 512:(m + 1) * 512])
        ab = dict(abufs)
        ab["pre"] = [("vector", mm, mm.n)]
        attn_run(C, AS, mk_iters_cross(qg, qg % 2), ab)
        wo_step(qg, qg % 2)

    xnt = xnq[0]
    for tt in range(4):
        ln_fm(C, 1,
              lambda c, _t, tt=tt: xresc(c)[:, tt * 512:(tt + 1) * 512],
              lambda c, _t: xnt[:, c * 512:(c + 1) * 512],
              1e-5, sq0, sq1, rows, bcbuf)
        for c in range(NCH):
            st.n += 16
            B.on("gpsimd", lambda e, t=evv.n: e.wait_ge(evv.h, t))
            B.on("gpsimd", lambda e, c=c, tt=tt: e.dma_start(
                out=xnod_t.ap()[c, :, 2 + tt * 512:2 + (tt + 1) * 512],
                in_=xnt[:, c * 512:(c + 1) * 512]).then_inc(st.h, 16))
    B.on("gpsimd", lambda e, t=AS.aexp.n: e.wait_ge(AS.aexp.h, t))
    for c in range(NCH):
        lx.n += 16
        B.on("gpsimd", lambda e, c=c: e.dma_start(
            out=xqb_one[:, c * 512:(c + 1) * 512],
            in_=xoh_d[c]).then_inc(lx.h, 16))
    B.on("vector", lambda e, t=lx.n: e.wait_ge(lx.h, t))
    B.on("tensor", lambda e, t=lx.n: e.wait_ge(lx.h, t))
    ln_fm(C, 1,
          lambda c, _t: xqb_one[:, c * 512:(c + 1) * 512],
          lambda c, _t: xnt[:, c * 512:(c + 1) * 512],
          1e-5, sq0, sq1, rows, bcbuf,
          gcol=lambda c: col("hm"), bcol=lambda c: col("zz"))
    for c in range(NCH):
        st.n += 16
        B.on("gpsimd", lambda e, t=evv.n: e.wait_ge(evv.h, t))
        B.on("gpsimd", lambda e, c=c: e.dma_start(
            out=xnod_t.ap()[c, :, 0:2],
            in_=xnt[:, c * 512:c * 512 + 2]).then_inc(st.h, 16))

    B.on("gpsimd", lambda e, t=st.n: e.wait_ge(st.h, t))
    if stop != "attn_nc":
        cc.n += 1
        B.on("gpsimd", lambda e: e.collective_compute(
            "ReduceScatter", ALU.add, replica_groups=PAIRS,
            ins=[ohp_t.ap().opt()], outs=[rso_t.ap().opt()]).then_inc(cc.h))
        B.on("gpsimd", lambda e, t=cc.n: e.wait_ge(cc.h, t))
        st_rb0 = st.n
        for j in range(16):
            stg = rowstage[:, (j % 4) * 1024:(j % 4 + 1) * 1024]
            if j >= 4:
                B.on("gpsimd", lambda e, t=st_rb0 + 16 * (j - 3): e.wait_ge(st.h, t))
            lx.n += 16
            B.on("gpsimd", lambda e, j=j, stg=stg: e.dma_start(
                out=stg, in_=rso_t.ap()[j * 128:(j + 1) * 128, :]).then_inc(lx.h, 16))
            B.on("gpsimd", lambda e, t=lx.n: e.wait_ge(lx.h, t))
            st.n += 16
            B.on("gpsimd", lambda e, j=j, stg=stg: e.dma_start(
                out=rsob_t.ap()[j * 128:(j + 1) * 128, :], in_=stg).then_inc(st.h, 16))
        B.on("sync", lambda e, t=st.n: e.wait_ge(st.h, t))
        lt = C.lt
        for c in range(NCH):
            aab = out_tile[c % 2][:, 0:S_OWN]
            lt.n += 16
            if c >= 2:
                B.on("sync", lambda e, t=evv.n: e.wait_ge(evv.h, t))
            B.on("sync", lambda e, c=c, aab=aab: e.dma_start_transpose(
                out=aab, in_=rsob_t.ap()[:, c * 128:(c + 1) * 128]).then_inc(lt.h, 16))
            B.on("vector", lambda e, t=lt.n: e.wait_ge(lt.h, t))
            evv.n += 1
            B.on("vector", lambda e, c=c, aab=aab: e.tensor_tensor(
                xresc(c), xresc(c), aab, ALU.add).then_inc(evv.h, 1))

    if stop in ("attn", "attn_nc"):
        C.bar_all()
        if dbg:
            dump_sb(B, C, "d_xres", xres, [128, NCH * S_OWN], F32)
            dohp = B.dram("d_ohp", [S_FULL, D], F32, "ExternalOutput").ap()
            C.st.n += 16
            B.on("gpsimd", lambda e: e.dma_start(
                out=dohp, in_=ohp_t.ap()).then_inc(C.st.h, 16))
            dump_sb(B, C, "d_ot", out_tile[1], [128, 4 * 512], BF16)
        finish_program(B, C)
        return nc, stack

    for c in range(NCH):
        st.n += 16
        B.on("gpsimd", lambda e, t=evv.n: e.wait_ge(evv.h, t))
        B.on("gpsimd", lambda e, c=c: e.dma_start(
            out=xpre_t.ap()[c], in_=xresc(c)).then_inc(st.h, 16))

    bt, ft = ABF.t, AF32.t
    ebc = lambda e2: ft[:, 8192 + e2 * 1024:8192 + (e2 + 1) * 1024]
    ewb = [bt[:, 8192:12288], bt[:, 14336:18432]]
    xnt2 = bt[:, 18432:22546]
    convb = bt[:, 22546:26642]
    x1b = bt[:, 26642:30738]
    accb = bt[:, 30738:34834]
    hb = bt[:, 34834:38930]
    fdbb = bt[:, 38930:39954]
    yeb = bt[:, 39954:40466]
    rwr_sb = ft[:, 23640:23704]
    x1f = ft[:, 0:4096]
    xrt = ft[:, 0:4096]
    sh = ft[:, 4096:8192]
    acct = ft[:, 21008:23056]
    logt = ft[:, 23056:23088]
    rbbc = ft[:, 23088:23096]
    convs = ft[:, 23096:23608]
    g8f = ft[:, 23608:23640]

    load_wsb(C, rwr_sb, rwr_d, 8, 8, queue="sync")

    B.on("tensor", lambda e, t=evv.n: e.wait_ge(evv.h, t))
    B.on("tensor", lambda e, t=evs.n: e.wait_ge(evs.h, t))
    B.on("tensor", lambda e, t=st.n: e.wait_ge(st.h, t))
    mm.n += 1
    B.on("tensor", lambda e: e.matmul(
        C.psf[4][:, 0:8], C.ones_row[:], rbrow[:], start=True, stop=True
        ).then_inc(mm.h, 1))
    B.on("vector", lambda e, t=mm.n: e.wait_ge(mm.h, t))
    evv.n += 1
    B.on("vector", lambda e: e.tensor_copy(rbbc, C.psf[4][:, 0:8]).then_inc(evv.h, 1))
    for e2 in range(8):
        lx.n += 16
        B.on("gpsimd", lambda e, e2=e2: e.dma_start(
            out=ebrow1[:], in_=ebrows_d[e2:e2 + 1, :]).then_inc(lx.h, 16))
        B.on("tensor", lambda e, t=lx.n: e.wait_ge(lx.h, t))
        B.on("tensor", lambda e, t=evv.n: e.wait_ge(evv.h, t))
        mm.n += 1

        def febc(e, e2=e2):
            e.matmul(C.psf[4][:, 0:512], C.ones_row[:], ebrow1[:, 0:512],
                     start=True, stop=True)
            e.matmul(C.psf[5][:, 0:512], C.ones_row[:], ebrow1[:, 512:1024],
                     start=True, stop=True).then_inc(mm.h, 1)
        B.on("tensor", febc)
        B.on("vector", lambda e, t=mm.n: e.wait_ge(mm.h, t))
        evv.n += 1

        def febe(e, e2=e2):
            e.tensor_copy(ebc(e2)[:, 0:512], C.psf[4][:, 0:512])
            e.tensor_copy(ebc(e2)[:, 512:1024],
                          C.psf[5][:, 0:512]).then_inc(evv.h, 1)
        B.on("vector", febe)

    for tt in range(4):
        B.on("gpsimd", lambda e, t=evv.n: e.wait_ge(evv.h, t))
        B.on("gpsimd", lambda e, t=evs.n: e.wait_ge(evs.h, t))
        B.on("gpsimd", lambda e, t=mm.n: e.wait_ge(mm.h, t))
        for c in range(NCH):
            lx.n += 16
            B.on("gpsimd", lambda e, c=c, tt=tt: e.dma_start(
                out=xnt2[:, c * 514:c * 514 + 514],
                in_=xnod_t.ap()[c, :, tt * 512:tt * 512 + 514]).then_inc(lx.h, 16))
        B.on("vector", lambda e, t=lx.n: e.wait_ge(lx.h, t))
        for c in range(NCH):
            evv.n += 1

            def fconv(e, c=c):
                xm2 = xnt2[:, c * 514:c * 514 + 512]
                xm1 = xnt2[:, c * 514 + 1:c * 514 + 513]
                x0 = xnt2[:, c * 514 + 2:c * 514 + 514]
                e.tensor_scalar(convs, xm2, col("dwk", c * 3 + 0), None, ALU.mult)
                e.scalar_tensor_tensor(convs, xm1, col("dwk", c * 3 + 1),
                                       convs, ALU.mult, ALU.add)
                e.scalar_tensor_tensor(convs, x0, col("dwk", c * 3 + 2),
                                       convs, ALU.mult, ALU.add)
                e.tensor_scalar(convb[:, c * 512:(c + 1) * 512], convs,
                                col("dwb", c), None,
                                ALU.add).then_inc(evv.h, 1)
            B.on("vector", fconv)
        B.on("tensor", lambda e, t=evv.n: e.wait_ge(evv.h, t))
        proj_fm(C, W["pww"], 8, 8,
                (lambda c, _t: convb[:, c * 512:(c + 1) * 512]), 1,
                (lambda kind, e, ps, m, _tt: (
                    e.activation(yeb[:], ps, AF.Silu, bias=col("pwb", m)),
                    e.copy(x1b[:, m * 512:(m + 1) * 512], yeb[:]))[-1]),
                wbuf, epi_eng=("scalar",))
        for c in range(NCH):
            B.on("vector", lambda e, t=evs.n - (NCH - 1 - c): e.wait_ge(evs.h, t))
            evv.n += 1

            def fx1(e, c=c):
                e.tensor_tensor(x1f[:, c * 512:(c + 1) * 512],
                                xnt2[:, c * 514 + 2:c * 514 + 514],
                                x1b[:, c * 512:(c + 1) * 512], ALU.add)
                e.tensor_copy(x1b[:, c * 512:(c + 1) * 512],
                              x1f[:, c * 512:(c + 1) * 512]).then_inc(evv.h, 1)
            B.on("vector", fx1)
        B.on("tensor", lambda e, t=evv.n: e.wait_ge(evv.h, t))
        B.on("tensor", lambda e, t=lw.n: e.wait_ge(lw.h, t))
        proj_tok(C, rwr_sb, 8, 8,
                 (lambda c, j: x1f[:, c * 512 + j * 128:c * 512 + (j + 1) * 128]),
                 4,
                 (lambda kind, e, ps, j: e.tensor_tensor(
                     logt[:, j * 8:(j + 1) * 8], ps, rbbc, ALU.add)),
                 epi_eng=("vector",))
        for j in range(4):
            lg = logt[:, j * 8:(j + 1) * 8]
            eq1 = sh[:, j * 64:j * 64 + 8]
            l2 = sh[:, j * 64 + 8:j * 64 + 16]
            eq2 = sh[:, j * 64 + 16:j * 64 + 24]
            t1_ = sh[:, j * 64 + 24:j * 64 + 32]
            t2_ = sh[:, j * 64 + 32:j * 64 + 40]
            gcj = g8f[:, j * 8:(j + 1) * 8]
            s0 = stat[:, 8:9]
            s1_ = stat[:, 9:10]
            s2_ = stat[:, 10:11]
            sc0 = stat[:, 12:13]
            sc1 = stat[:, 13:14]
            evv.n += 1
            B.on("vector", lambda e, lg=lg, s0=s0: e.reduce_max(
                s0, lg, axis=AX).then_inc(evv.h, 1))
            B.on("gpsimd", lambda e, t=evv.n: e.wait_ge(evv.h, t))
            gp.n += 1
            B.on("gpsimd", lambda e, lg=lg, s0=s0, eq1=eq1: e.tensor_scalar(
                eq1, lg, s0, None, ALU.is_ge).then_inc(gp.h, 1))
            B.on("vector", lambda e, t=gp.n: e.wait_ge(gp.h, t))
            evv.n += 1
            B.on("vector", lambda e, lg=lg, eq1=eq1, l2=l2: e.scalar_tensor_tensor(
                l2, eq1, NEG, lg, ALU.mult, ALU.add).then_inc(evv.h, 1))
            B.on("gpsimd", lambda e, t=evv.n: e.wait_ge(evv.h, t))
            l2g = sh[:, j * 64 + 40:j * 64 + 48]
            gp.n += 1
            B.on("gpsimd", lambda e, l2=l2, l2g=l2g: e.tensor_copy(
                l2g, l2).then_inc(gp.h, 1))
            B.on("vector", lambda e, t=gp.n: e.wait_ge(gp.h, t))
            evv.n += 1
            B.on("vector", lambda e, l2g=l2g, s1_=s1_: e.reduce_max(
                s1_, l2g, axis=AX).then_inc(evv.h, 1))
            B.on("gpsimd", lambda e, t=evv.n: e.wait_ge(evv.h, t))
            gp.n += 1

            def feq2(e, l2=l2, s1_=s1_, eq2=eq2, s0=s0, s2_=s2_):
                e.tensor_scalar(eq2, l2, s1_, None, ALU.is_ge)
                e.tensor_tensor(s2_, s1_, s0, ALU.subtract).then_inc(gp.h, 1)
            B.on("gpsimd", feq2)
            B.on("scalar", lambda e, t=gp.n: e.wait_ge(gp.h, t))
            evs.n += 1
            B.on("scalar", lambda e, s2_=s2_, sc1=sc1: e.activation(
                sc1, s2_, AF.Sigmoid).then_inc(evs.h, 1))
            B.on("gpsimd", lambda e, t=evs.n: e.wait_ge(evs.h, t))
            gp.n += 1
            B.on("gpsimd", lambda e, sc0=sc0, sc1=sc1: e.tensor_scalar(
                sc0, sc1, -1.0, 1.0, ALU.mult, ALU.add).then_inc(gp.h, 1))
            B.on("vector", lambda e, t=gp.n: e.wait_ge(gp.h, t))
            evv.n += 1

            def ft12(e, eq1=eq1, eq2=eq2, sc0=sc0, sc1=sc1, t1_=t1_, t2_=t2_):
                e.tensor_scalar(t1_, eq1, sc0, None, ALU.mult)
                e.tensor_scalar(t2_, eq2, sc1, None, ALU.mult).then_inc(evv.h, 1)
            B.on("vector", ft12)
            B.on("gpsimd", lambda e, t=evv.n: e.wait_ge(evv.h, t))
            gp.n += 1
            B.on("gpsimd", lambda e, t1_=t1_, t2_=t2_, gcj=gcj: e.tensor_tensor(
                gcj, t1_, t2_, ALU.add).then_inc(gp.h, 1))
        if stop == "gates" and tt == 0:
            C.bar_all()
            if dbg:
                dump_sb(B, C, "d_x1", x1f, [128, 4096], F32)
                dump_sb(B, C, "d_g8", g8f, [128, 32], F32)
                dump_sb(B, C, "d_lg", logt, [128, 32], F32)
            finish_program(B, C)
            return nc, stack
        acctj = lambda j: ft[:, 21008 + (j % 2) * 1024:21008 + (j % 2 + 1) * 1024]
        convs2 = [ft[:, 23704 + k * 512:23704 + (k + 1) * 512] for k in range(2)]
        yeb2 = [yeb, bt[:, 40530:41042]]
        for jg in range(2):
            gi = 0
            bias_incs = []
            lw0e = lw.n
            mm0e = mm.n
            for e2 in range(8):
                for half in range(2):
                    wi = e2 * 2 + half
                    if wi >= 2:
                        B.on("sync", lambda e, t=mm0e + (wi - 1) * 2:
                             e.wait_ge(mm.h, t))
                    else:
                        B.on("sync", lambda e, t=mm0e: e.wait_ge(mm.h, t))
                    lw.n += 16
                    B.on("sync", lambda e, e2=e2, half=half, wi=wi: e.dma_start(
                        out=ewb[wi % 2].rearrange("p (k m) -> p k m", k=8),
                        in_=ewr_d[e2, :, :, half * 512:(half + 1) * 512]
                        .rearrange("k p m -> p k m")).then_inc(lw.h, 16))
                    B.on("tensor", lambda e, t=lw.n: e.wait_ge(lw.h, t))
                    for j in (jg * 2, jg * 2 + 1):
                        gi += 1
                        bank = C.psf[(gi - 1) % 4]
                        if gi > 4:
                            B.on("tensor", lambda e, t=bias_incs[gi - 5]:
                                 e.wait_ge(evv.h, t))
                        mm.n += 1

                        def fem(e, j=j, wi=wi, bank=bank):
                            inst = None
                            for c in range(NCH):
                                inst = e.matmul(
                                    bank[:, 0:512],
                                    x1b[:, c * 512 + j * 128:c * 512 + (j + 1) * 128],
                                    ewb[wi % 2][:, c * 512:(c + 1) * 512],
                                    start=(c == 0), stop=(c == NCH - 1))
                            inst.then_inc(mm.h, 1)
                        B.on("tensor", fem)
                        par2 = (gi - 1) % 2
                        B.on("vector", lambda e, t=mm.n: e.wait_ge(mm.h, t))
                        if gi > 2:
                            B.on("vector", lambda e, t=evs.n: None)
                        evv.n += 1
                        bias_incs.append(evv.n)
                        B.on("vector", lambda e, bank=bank, e2=e2, half=half,
                             par2=par2: e.tensor_tensor(
                                 convs2[par2], bank[:, 0:512],
                                 ebc(e2)[:, half * 512:(half + 1) * 512],
                                 ALU.add).then_inc(evv.h, 1))
                        B.on("scalar", lambda e, t=evv.n: e.wait_ge(evv.h, t))
                        evs.n += 1
                        B.on("scalar", lambda e, par2=par2: e.activation(
                            yeb2[par2], convs2[par2],
                            AF.Silu).then_inc(evs.h, 1))
                        B.on("vector", lambda e, t=evs.n: e.wait_ge(evs.h, t))
                        evv.n += 1

                        def facc(e, j=j, e2=e2, half=half, par2=par2):
                            gcol = g8f[:, j * 8 + e2:j * 8 + e2 + 1]
                            dst = acctj(j)[:, half * 512:(half + 1) * 512]
                            if e2 == 0:
                                e.tensor_scalar(dst, yeb2[par2], gcol, None,
                                                ALU.mult).then_inc(evv.h, 1)
                            else:
                                e.scalar_tensor_tensor(
                                    dst, yeb2[par2], gcol, dst, ALU.mult,
                                    ALU.add).then_inc(evv.h, 1)
                        B.on("vector", facc)
            for j in (jg * 2, jg * 2 + 1):
                evv.n += 1
                B.on("vector", lambda e, j=j: e.tensor_copy(
                    accb[:, (j % 2) * 1024 + 2048:0] if False else
                    bt[:, 41042 + (j % 2) * 1024:41042 + (j % 2 + 1) * 1024],
                    acctj(j)).then_inc(evv.h, 1))
                B.on("tensor", lambda e, t=evv.n: e.wait_ge(evv.h, t))
                for bb in range(4):
                    half2 = AS.tb % 2
                    if AS.tb >= 2:
                        B.on("tensor", lambda e, t=AS.tb - 1: e.wait_ge(AS.aevt.h, t))
                    AS.tb += 1
                    AS.amt.n += 1

                    def ftre(e, j=j, bb=bb, half2=half2):
                        src_ = bt[:, 41042 + (j % 2) * 1024:41042 + (j % 2 + 1) * 1024]
                        e.transpose(C.psbs[half2][:, 0:128],
                                    src_[:, bb * 256:bb * 256 + 128], C.ident[:])
                        e.transpose(C.psbs[half2][:, 128:256],
                                    src_[:, bb * 256 + 128:bb * 256 + 256],
                                    C.ident[:]).then_inc(AS.amt.h, 1)
                    B.on("tensor", ftre)
                    B.on("scalar", lambda e, t=AS.amt.n: e.wait_ge(AS.amt.h, t))
                    AS.aevt.n += 1

                    def fevte(e, j=j, bb=bb, half2=half2):
                        c0 = bb * 2
                        e.copy(accb[:, c0 * 512 + j * 128:c0 * 512 + j * 128 + 128],
                               C.psbs[half2][:, 0:128])
                        e.copy(accb[:, (c0 + 1) * 512 + j * 128:
                                    (c0 + 1) * 512 + j * 128 + 128],
                               C.psbs[half2][:, 128:256]).then_inc(AS.aevt.h, 1)
                    B.on("scalar", fevte)
        B.on("tensor", lambda e, t=AS.aevt.n: e.wait_ge(AS.aevt.h, t))
        proj_fm(C, W["oww"], 8, 8,
                (lambda c, _t: accb[:, c * 512:(c + 1) * 512]), 1,
                (lambda kind, e, ps, m, _tt: e.scalar_tensor_tensor(
                    sh[:, m * 512:(m + 1) * 512], ps, col("owb", m),
                    x1f[:, m * 512:(m + 1) * 512], ALU.add, ALU.add)),
                wbuf, epi_eng=("vector",))
        B.on("gpsimd", lambda e, t=evv.n: e.wait_ge(evv.h, t))
        for c in range(NCH):
            lx.n += 16
            B.on("gpsimd", lambda e, c=c, tt=tt: e.dma_start(
                out=xrt[:, c * 512:(c + 1) * 512],
                in_=xpre_t.ap()[c, :, tt * 512:(tt + 1) * 512]).then_inc(lx.h, 16))
        mm0r = mm.n
        for c in range(NCH):
            if c >= 2:
                B.on("vector", lambda e, t=mm0r + c - 1: e.wait_ge(mm.h, t))
            evv.n += 1
            B.on("vector", lambda e, c=c: e.tensor_tensor(
                sq[c % 2][:] if False else (sq0 if c % 2 == 0 else sq1)[:],
                sh[:, c * 512:(c + 1) * 512], sh[:, c * 512:(c + 1) * 512],
                ALU.mult).then_inc(evv.h, 1))
            mm.n += 1
            B.on("tensor", lambda e, t=evv.n: e.wait_ge(evv.h, t))
            B.on("tensor", lambda e, c=c: e.matmul(
                C.psf[5][0:1, 0:512], C.ones_col[:],
                (sq0 if c % 2 == 0 else sq1)[:],
                start=(c == 0), stop=(c == NCH - 1)).then_inc(mm.h, 1))
        B.on("vector", lambda e, t=mm.n: e.wait_ge(mm.h, t))
        evv.n += 1
        B.on("vector", lambda e: (
            e.tensor_scalar(rows[0:1, 0:512], C.psf[5][0:1, 0:512],
                            1.0 / D, 1e-6, ALU.mult, ALU.add),
            e.tensor_copy(rows[0:1, 512:1024],
                          rows[0:1, 0:512]).then_inc(evv.h, 1)))
        B.on("scalar", lambda e, t=evv.n: e.wait_ge(evv.h, t))
        evs.n += 1
        B.on("scalar", lambda e: e.activation(
            rows[0:1, 512:1024], rows[0:1, 512:1024],
            AF.Sqrt).then_inc(evs.h, 1))
        B.on("vector", lambda e, t=evs.n: e.wait_ge(evs.h, t))
        evv.n += 1
        B.on("vector", lambda e: e.reciprocal(
            rows[0:1, 1024:1536], rows[0:1, 512:1024]).then_inc(evv.h, 1))
        B.on("tensor", lambda e, t=evv.n: e.wait_ge(evv.h, t))
        mm.n += 1
        B.on("tensor", lambda e: e.matmul(
            C.psf[4][:, 0:512], C.ones_row[:], rows[0:1, 1024:1536],
            start=True, stop=True).then_inc(mm.h, 1))
        B.on("vector", lambda e, t=mm.n: e.wait_ge(mm.h, t))
        evv.n += 1
        B.on("vector", lambda e: e.tensor_copy(
            bcbuf[:, 0:512], C.psf[4][:, 0:512]).then_inc(evv.h, 1))
        for c in range(NCH):
            evv.n += 1

            def fh(e, c=c):
                e.tensor_tensor(sh[:, c * 512:(c + 1) * 512],
                                sh[:, c * 512:(c + 1) * 512], bcbuf[:, 0:512],
                                ALU.mult)
                e.tensor_scalar(sh[:, c * 512:(c + 1) * 512],
                                sh[:, c * 512:(c + 1) * 512], col("nw", c),
                                None, ALU.mult)
                e.tensor_copy(hb[:, c * 512:(c + 1) * 512],
                              sh[:, c * 512:(c + 1) * 512]).then_inc(evv.h, 1)
            B.on("vector", fh)
        if stop == "moe" and tt == 0:
            C.bar_all()
            if dbg:
                dump_sb(B, C, "d_g8", g8f, [128, 32], F32)
                dump_sb(B, C, "d_acc", accb, [128, 4096], BF16)
                dump_sb(B, C, "d_h", sh, [128, 4096], F32)
            finish_program(B, C)
            return nc, stack
        B.on("tensor", lambda e, t=evv.n: e.wait_ge(evv.h, t))
        proj_fm(C, W["fdw"], 2, 8,
                (lambda c, _t: hb[:, c * 512:(c + 1) * 512]), 1,
                (lambda kind, e, ps, m, _tt: e.activation(
                    fdbb[:, m * 512:(m + 1) * 512], ps, AF.Gelu,
                    bias=col("fdb", m))), wbuf, epi_eng=("scalar",))
        B.on("tensor", lambda e, t=evs.n: e.wait_ge(evs.h, t))
        B.on("vector", lambda e, t=lx.n: e.wait_ge(lx.h, t))

        def ffu_epi(kind, e, ps, m, _tt, tt=tt):
            e.scalar_tensor_tensor(convs2[0], ps, col("fub", m),
                                   sh[:, m * 512:(m + 1) * 512],
                                   ALU.add, ALU.add)
            return e.tensor_tensor(xrt[:, m * 512:(m + 1) * 512],
                                   convs2[0], xrt[:, m * 512:(m + 1) * 512],
                                   ALU.add)
        proj_fm(C, W["fuw"], 8, 2,
                (lambda c, _t: fdbb[:, c * 512:(c + 1) * 512]), 1,
                ffu_epi, wbuf, epi_eng=("vector",))
        for c in range(NCH):
            B.on("gpsimd", lambda e, t=evv.n - (NCH - 1 - c): e.wait_ge(evv.h, t))
            st.n += 16
            B.on("gpsimd", lambda e, c=c, tt=tt: e.dma_start(
                out=x1o_d[c, :, tt * 512:(tt + 1) * 512],
                in_=xrt[:, c * 512:(c + 1) * 512]).then_inc(st.h, 16))

    if stop in ("all", "ffn"):
        finish_program(B, C)
        return nc, stack



def finish_program(B: Builder, C: Ctx):
    B.on("gpsimd", lambda e, t=C.st.n: e.wait_ge(C.st.h, t))
    B.finish()
    B.nc._B = B



def alibi_slopes():
    return np.array([2.0 ** (-0.5 * (h + 1)) for h in range(16)], np.float32)


def causal_diag_masks():
    out = np.zeros((128, 4 * 512), np.float32)
    i = np.arange(128)[:, None]
    j = np.arange(512)[None, :]
    for v in range(4):
        out[:, v * 512:(v + 1) * 512] = np.where(j <= v * 128 + i, 0.0, NEG)
    return out


def stage_A(ins, core):
    b, g = core // 2, core % 2
    x = np.asarray(ins["L0_attn_hidden_states"][b], np.float32)
    own = x[g * S_OWN:(g + 1) * S_OWN]
    heads = slice(g * 512, (g + 1) * 512)

    halo = np.zeros((2, D), np.float32)
    if g == 1:
        halo[:] = x[S_OWN - 2:S_OWN]
    halo512 = np.tile(halo, (256, 1))

    cols = np.zeros((128, 160), np.float32)

    def put(nm, arr, off_map={}):
        o = STAGE_COLS[nm]
        arr = np.asarray(arr, np.float32)
        cols[:, o:o + arr.shape[1]] = arr

    put("cb1", stage_col(ins["L0_attn_b1"]))
    put("cb2", stage_col(ins["L0_attn_b2"]))
    put("dwb", stage_col(ins["L0_ffn_dw_bias"]))
    put("pwb", stage_col(ins["L0_ffn_pw_b"]))
    put("owb", stage_col(ins["L0_ffn_out_b"]))
    put("nw", stage_col(ins["L0_ffn_norm_w"]))
    put("fdb", stage_col(ins["L0_ffn_ff_down_b"]))
    put("fub", stage_col(ins["L0_ffn_ff_up_b"]))
    cols[:, STAGE_COLS["hm"]] = float(g)
    cols[:, STAGE_COLS["zz"]] = 0.0
    dwk = np.asarray(ins["L0_ffn_dw_kernel"], np.float32)[:, 0, :]
    dwkc = dwk.reshape(NCH, 128, 3).transpose(1, 0, 2).reshape(128, 24)
    cols[:, STAGE_COLS["dwk"]:STAGE_COLS["dwk"] + 24] = dwkc
    eb = np.asarray(ins["L0_ffn_expert_b"], np.float32)
    for e in range(8):
        cols[:, STAGE_COLS["eb"] + e * 8:STAGE_COLS["eb"] + (e + 1) * 8] = \
            stage_col(eb[e])

    exw = np.asarray(ins["L0_ffn_expert_w"], np.float32)
    ewr = np.stack([stage_w_rhs(exw[e].T) for e in range(8)])

    im = dict(
        ident=np.eye(128, dtype=np.float32).astype(bf),
        xo=fm(own),
        xoh=fm(halo512),
        xq=fm(x),
        cdiag=causal_diag_masks().astype(bf),
        cols=cols,
        rbrow=np.asarray(ins["L0_ffn_router_b"], np.float32).reshape(1, 8),
        wq=stage_w(np.asarray(ins["L0_attn_Wq"])[:, heads]),
        wk=stage_w(np.asarray(ins["L0_attn_Wk"])[:, heads]),
        wo=stage_w(np.asarray(ins["L0_attn_Wo"])[g * 512:(g + 1) * 512, :]),
        cw1=stage_w(ins["L0_attn_W1"]),
        cw2=stage_w(ins["L0_attn_W2"]),
        pww=stage_w(np.asarray(ins["L0_ffn_pw_w"]).T),
        oww=stage_w(np.asarray(ins["L0_ffn_out_w"]).T),
        fdw=stage_w(np.asarray(ins["L0_ffn_ff_down_w"]).T),
        fuw=stage_w(np.asarray(ins["L0_ffn_ff_up_w"]).T),
        ewr=ewr,
        wvr=stage_w_rhs(np.asarray(ins["L0_attn_Wv"])[:, heads]),
        rwr=np.ascontiguousarray(np.asarray(ins["L0_ffn_router_w"], np.float32).T.reshape(8, 128, 8)),
        ebrows=np.asarray(ins["L0_ffn_expert_b"], np.float32),
    )
    return im


STAGE_COLS = {}
_off = 0
for _nm, _n in [("cb1", 4), ("cb2", 8), ("dwb", 8), ("pwb", 8),
                ("owb", 8), ("nw", 8), ("fdb", 2), ("fub", 8),
                ("dwk", 24), ("eb", 64), ("hm", 1), ("zz", 1)]:
    STAGE_COLS[_nm] = _off
    _off += _n


def stage_w_rhs(Wf):
    Wf = np.asarray(Wf, np.float32)
    di, no = Wf.shape
    assert di % 128 == 0
    return np.ascontiguousarray(Wf.reshape(di // 128, 128, no)).astype(bf)



def build_B(stop="all", dbg=False, drc=True):
    stack = ExitStack()
    nc = bass.Bass("TRN2", target_bir_lowering=False, detect_race_conditions=drc)
    B = Builder(nc, stack)
    C = Ctx(B)
    AS = AttnSems(B)
    ap = lambda t: t.ap()

    ident_d = ap(B.dram("ident", [128, 128], BF16, "ExternalInput"))
    x1_d = ap(B.dram("x1", [NCH, 128, S_OWN], F32, "ExternalInput"))
    cdiag_d = ap(B.dram("cdiag", [128, 4 * 512], BF16, "ExternalInput"))
    iota_d = ap(B.dram("iota512", [1, 512], F32, "ExternalInput"))
    wqkvr_d = ap(B.dram("wqkvr", [6, 8, 128, 512], BF16, "ExternalInput"))
    wo_d = ap(B.dram("wo", [8, 4, 128, 128], BF16, "ExternalInput"))
    bout_d = ap(B.dram("bout", [128, 8], F32, "ExternalInput"))
    slp_d = ap(B.dram("slp", [1, 8], F32, "ExternalInput"))

    x2p_d = ap(B.dram("x2p", [NCH, 128, S_OWN], F32, "ExternalOutput"))
    xn1o_d = ap(B.dram("xn1o", [NCH, 128, S_OWN], F32, "ExternalOutput"))

    qa_d = ap(B.dram("qa", [S_FULL, 512], BF16, "ExternalInput"))
    ka_d = ap(B.dram("ka", [S_FULL, 512], BF16, "ExternalInput"))
    va_d = ap(B.dram("va", [S_FULL, 512], BF16, "ExternalInput"))
    qkvb_t = ap(B.dram("qkvb", [S_OWN, 3072], BF16, "ExternalOutput"))
    _unused_qkvag = None
    ohp_t = B.dram("ohp", [S_FULL, D], F32)
    rso_t = B.dram("rso", [S_OWN, D], F32)
    rsob_t = B.dram("rsob", [S_OWN, D], BF16)

    AF32 = Arena(B, "af32", 16400, F32)
    ABF = Arena(B, "abf", 67584, BF16)
    lx, mm, evv, evs, gp, st, cc, lw, lt = (C.lx, C.mm, C.evv, C.evs, C.gp,
                                            C.st, C.cc, C.lw, C.lt)
    C.init_consts(ident_d)

    sq0 = AF32.alloc(512)
    sq1 = AF32.alloc(512)
    bcbuf = AF32.alloc(1024)
    rows = AF32.alloc(2048)
    stat = AF32.alloc(16)
    jrow = AF32.alloc(4096)
    xt_one = AF32.alloc(NCH * 512)
    xt = [xt_one, xt_one]
    bout = B.sbuf("bout_sb", [128, 8], F32)
    cdiag = B.sbuf("cdiag_sb", [128, 4 * 512], BF16)
    iota5 = B.sbuf("iota5", [1, 512], F32)
    slp_row = B.sbuf("slp_sb", [1, 8], F32)

    xn1t = [ABF.alloc(NCH * 512) for _ in range(2)]
    Kf = ABF.alloc(4 * S_FULL)
    Vt = ABF.alloc(32 * 512)
    Qg = [ABF.alloc(4 * 512) for _ in range(2)]
    P1 = ABF.alloc(S_FULL)
    PT1 = ABF.alloc(S_FULL)
    P_bf = [P1, P1]
    PT_sb = [PT1, PT1]
    out_tile = [ABF.alloc(4 * 512) for _ in range(2)]
    rowstage = ABF.alloc(4 * 1024)
    wbuf = ABF.alloc(2 * 8 * 128)
    wsb = ABF.alloc(8 * 512)
    S1 = AF32.alloc(S_FULL)
    S_sb = [S1, S1]

    Kc = lambda c: Kf[:, c * S_FULL:(c + 1) * S_FULL]

    lx.n += 16 * 4
    B.on("gpsimd", lambda e: e.dma_start(out=slp_row[:], in_=slp_d).then_inc(lx.h, 16))
    B.on("gpsimd", lambda e: e.dma_start(out=bout[:], in_=bout_d).then_inc(lx.h, 16))
    B.on("gpsimd", lambda e: e.dma_start(out=cdiag[:], in_=cdiag_d).then_inc(lx.h, 16))
    B.on("gpsimd", lambda e: e.dma_start(out=iota5[:], in_=iota_d).then_inc(lx.h, 16))
    ld0 = lx.n
    for eng in ("vector", "scalar", "tensor"):
        B.on(eng, lambda e, t=ld0: e.wait_ge(lx.h, t))

    qsg4 = [rowstage[:, k * 512:(k + 1) * 512] for k in range(4)]
    for tt in range(4):
        buf = tt % 2
        if tt >= 2:
            B.on("gpsimd", lambda e, t=evv.n: e.wait_ge(evv.h, t))
            B.on("gpsimd", lambda e, t=evs.n: e.wait_ge(evs.h, t))
            B.on("gpsimd", lambda e, t=mm.n: e.wait_ge(mm.h, t))
        for c in range(NCH):
            lx.n += 16
            B.on("gpsimd", lambda e, c=c, tt=tt: e.dma_start(
                out=xt_one[:, c * 512:(c + 1) * 512],
                in_=x1_d[c, :, tt * 512:(tt + 1) * 512]).then_inc(lx.h, 16))
        B.on("vector", lambda e, t=lx.n: e.wait_ge(lx.h, t))
        B.on("tensor", lambda e, t=lx.n: e.wait_ge(lx.h, t))
        ln_fm(C, 1,
              lambda c, _t: xt_one[:, c * 512:(c + 1) * 512],
              lambda c, _t, buf=buf: xn1t[buf][:, c * 512:(c + 1) * 512],
              1e-5, sq0, sq1, rows, bcbuf)
        for c in range(NCH):
            evv.n += 1
            B.on("vector", lambda e, c=c, buf=buf: e.tensor_copy(
                xt_one[:, c * 512:(c + 1) * 512],
                xn1t[buf][:, c * 512:(c + 1) * 512]).then_inc(evv.h, 1))
            st.n += 16
            B.on("gpsimd", lambda e, t=evv.n: e.wait_ge(evv.h, t))
            B.on("gpsimd", lambda e, c=c, tt=tt: e.dma_start(
                out=xn1o_d[c, :, tt * 512:(tt + 1) * 512],
                in_=xt_one[:, c * 512:(c + 1) * 512]).then_inc(st.h, 16))
        for nt in range(6):
            B.on("sync", lambda e, t=mm.n: e.wait_ge(mm.h, t))
            load_wsb(C, wsb, wqkvr_d[nt], 8, 512)
            B.on("tensor", lambda e, t=lw.n: e.wait_ge(lw.h, t))
            B.on("tensor", lambda e, t=evv.n: e.wait_ge(evv.h, t))
            B.on("tensor", lambda e, t=evs.n: e.wait_ge(evs.h, t))
            st0q = st.n
            ep_cnt = []
            for j in range(4):
                bank = C.psf[j % 4]
                mm.n += 1

                def fq(e, j=j, bank=bank, buf=buf):
                    inst = None
                    for c in range(NCH):
                        inst = e.matmul(bank[:, 0:512],
                                        xn1t[buf][:, c * 512 + j * 128:
                                                  c * 512 + (j + 1) * 128],
                                        wsb[:, c * 512:(c + 1) * 512],
                                        start=(c == 0), stop=(c == NCH - 1))
                    inst.then_inc(mm.h, 1)
                B.on("tensor", fq)
                kind = "vector" if j % 2 == 0 else "scalar"
                sem = evv if kind == "vector" else evs
                B.on(kind, lambda e, t=mm.n: e.wait_ge(mm.h, t))
                sem.n += 1
                ep_cnt.append((sem, sem.n))
                B.on(kind, lambda e, kind=kind, j=j, bank=bank, s_=sem: copy_any(
                    kind, e, qsg4[j % 4], bank[:, 0:512]).then_inc(s_.h, 1))
                B.on("gpsimd", lambda e, s_=sem, t=sem.n: e.wait_ge(s_.h, t))
                st.n += 16
                B.on("gpsimd", lambda e, j=j, nt=nt, tt=tt: e.dma_start(
                    out=qkvb_t[tt * 512 + j * 128:tt * 512 + (j + 1) * 128,
                                    nt * 512:(nt + 1) * 512],
                    in_=qsg4[j % 4]).then_inc(st.h, 16))
            B.on("tensor", lambda e, t=st.n: e.wait_ge(st.h, t))

    if stop == "b0":
        finish_program(B, C)
        return nc, stack

    B.on("gpsimd", lambda e, t=st.n: e.wait_ge(st.h, t))
    for c in range(4):
        lt.n += 16
        B.on("sync", lambda e, c=c: e.dma_start_transpose(
            out=Kc(c), in_=ka_d[:, c * 128:(c + 1) * 128]).then_inc(lt.h, 16))
    for j in range(32):
        lx.n += 16
        B.on("gpsimd", lambda e, j=j: e.dma_start(
            out=Vt[:, j * 512:(j + 1) * 512],
            in_=va_d[j * 128:(j + 1) * 128, :]).then_inc(lx.h, 16))
    kv_ld = (lt.n, lx.n)
    B.on("tensor", lambda e, t=kv_ld[0]: e.wait_ge(lt.h, t))
    B.on("tensor", lambda e, t=kv_ld[1]: e.wait_ge(lx.h, t))

    if stop == "kv":
        C.bar_all()
        if dbg:
            dump_sb(B, C, "d_kf", Kf, [128, 4 * S_FULL], BF16)
            dump_sb(B, C, "d_vt", Vt, [128, 32 * 512], BF16)
        finish_program(B, C)
        return nc, stack

    abufs = dict(S_sb=S_sb, P_bf=P_bf, PT_sb=PT_sb, stat=stat, serialize=True)

    def wo_stepB(qg, par):
        mm0 = mm.n
        lw0 = lw.n
        st0 = st.n
        B.on("tensor", lambda e, t=AS.aeo.n: e.wait_ge(AS.aeo.h, t))
        B.on("tensor", lambda e, t=evv.n: e.wait_ge(evv.h, t))
        B.on("tensor", lambda e, t=evs.n: e.wait_ge(evs.h, t))
        for m in range(8):
            if m >= 2:
                B.on("sync", lambda e, t=mm0 + m - 1: e.wait_ge(mm.h, t))
            else:
                B.on("sync", lambda e, t=mm0: e.wait_ge(mm.h, t))
            b = (m % 2) * 4 * 128
            lw.n += 16
            B.on("sync", lambda e, m=m, b=b: e.dma_start(
                out=wbuf[:, b:b + 4 * 128].rearrange("p (c q) -> p c q", c=4),
                in_=wo_d[m].rearrange("c p q -> p c q")).then_inc(lw.h, 16))
        for m in range(8):
            bank = C.psf[4 + m % 2]
            B.on("tensor", lambda e, t=lw0 + 16 * (m + 1): e.wait_ge(lw.h, t))
            if m >= 2:
                B.on("tensor", lambda e, t=evs.n - 1: e.wait_ge(evs.h, t))
            b = (m % 2) * 4 * 128
            mm.n += 1

            def fwo(e, m=m, b=b, bank=bank, par=par):
                inst = None
                for c in range(4):
                    inst = e.matmul(bank[:, 0:512],
                                    wbuf[:, b + c * 128:b + (c + 1) * 128],
                                    out_tile[par][:, c * 512:(c + 1) * 512],
                                    start=(c == 0), stop=(c == 3))
                inst.then_inc(mm.h, 1)
            B.on("tensor", fwo)
            B.on("scalar", lambda e, t=mm.n: e.wait_ge(mm.h, t))
            B.on("scalar", lambda e, t=st0: e.wait_ge(st.h, t))
            evs.n += 1
            B.on("scalar", lambda e, bank=bank: e.copy(
                wsb[:, 0:512], bank[:, 0:512]).then_inc(evs.h, 1))
            B.on("tensor", lambda e, t=evs.n: e.wait_ge(evs.h, t))
            for bb in range(2):
                half = AS.tb % 2
                if AS.tb >= 2:
                    B.on("tensor", lambda e, t=AS.tb - 1: e.wait_ge(AS.aevt.h, t))
                AS.tb += 1
                AS.amt.n += 1

                def ftr(e, bb=bb, half=half):
                    e.transpose(C.psbs[half][:, 0:128],
                                wsb[:, bb * 256:bb * 256 + 128], C.ident[:])
                    e.transpose(C.psbs[half][:, 128:256],
                                wsb[:, bb * 256 + 128:bb * 256 + 256],
                                C.ident[:]).then_inc(AS.amt.h, 1)
                B.on("tensor", ftr)
                B.on("scalar", lambda e, t=AS.amt.n: e.wait_ge(AS.amt.h, t))
                AS.aevt.n += 1

                def fevt(e, bb=bb, half=half, m=m):
                    j0 = bb * 2
                    e.copy(rowstage[:, j0 * 1024 + m * 128:j0 * 1024 + m * 128 + 128],
                           C.psbs[half][:, 0:128])
                    e.copy(rowstage[:, (j0 + 1) * 1024 + m * 128:
                                    (j0 + 1) * 1024 + m * 128 + 128],
                           C.psbs[half][:, 128:256]).then_inc(AS.aevt.h, 1)
                B.on("scalar", fevt)
        B.on("gpsimd", lambda e, t=AS.aevt.n: e.wait_ge(AS.aevt.h, t))
        for j in range(4):
            st.n += 16
            B.on("gpsimd", lambda e, j=j, qg=qg: e.dma_start(
                out=ohp_t.ap()[qg * 512 + j * 128:qg * 512 + (j + 1) * 128, :],
                in_=rowstage[:, j * 1024:(j + 1) * 1024]).then_inc(st.h, 16))

    for qg in range(8):
        par = qg % 2
        B.on("sync", lambda e, t=AS.asm.n: e.wait_ge(AS.asm.h, t))
        for c in range(4):
            lt.n += 16
            B.on("sync", lambda e, c=c, qg=qg, par=par: e.dma_start_transpose(
                out=Qg[par][:, c * 512:(c + 1) * 512],
                in_=qa_d[qg * 512:(qg + 1) * 512, c * 128:(c + 1) * 128]
            ).then_inc(lt.h, 16))
        qld = lt.n
        B.on("tensor", lambda e, t=qld: e.wait_ge(lt.h, t))
        for h in range(HEADS_OWN):
            B.on("vector", lambda e, t=AS.asm.n: e.wait_ge(AS.asm.h, t))
            evv.n += 1

            def fjr(e, h=h):
                inst = None
                for w in range(8):
                    inst = e.tensor_scalar(
                        jrow[0:1, w * 512:(w + 1) * 512], iota5[:],
                        float(w * 512), slp_row[0:1, h:h + 1],
                        ALU.add, ALU.mult)
                inst.then_inc(evv.h, 1)
            B.on("vector", fjr)
            iters = []
            for qti in range(4):
                qt = qg * 4 + qti
                nwin = qt // 4 + 1
                ch, base = h // 2, (h % 2) * 64
                wins = []
                wlens = []
                jmm = []
                for w in range(nwin):
                    diag = (w == qt // 4)
                    masks = []
                    if diag:
                        masks.append(("mask", cdiag[:, (qt % 4) * 512:
                                                    (qt % 4 + 1) * 512]))
                    wins.append((Kc(ch)[base:base + 64, w * 512:(w + 1) * 512],
                                 masks))
                    wlens.append(512)
                    jmm.append((C.ones_row[:],
                                jrow[0:1, w * 512:(w + 1) * 512]))
                iters.append(dict(
                    q=Qg[par][base:base + 64, ch * 512 + qti * 128:
                              ch * 512 + (qti + 1) * 128],
                    wins=wins, wlens=wlens, scale=0.125, jmm=jmm,
                    v=[Vt[:, j * 512 + h * 64:j * 512 + h * 64 + 64]
                       for j in range(nwin * 4)],
                    out=out_tile[par][base:base + 64,
                                      ch * 512 + qti * 128:
                                      ch * 512 + (qti + 1) * 128],
                ))
            ab = dict(abufs)
            ab["pre"] = [("tensor", evv, evv.n), ("vector", mm, mm.n)]
            attn_run(C, AS, iters, ab)
        wo_stepB(qg, par)

    B.on("gpsimd", lambda e, t=st.n: e.wait_ge(st.h, t))
    cc.n += 1
    B.on("gpsimd", lambda e: e.collective_compute(
        "ReduceScatter", ALU.add, replica_groups=PAIRS,
        ins=[ohp_t.ap().opt()], outs=[rso_t.ap().opt()]).then_inc(cc.h))
    B.on("gpsimd", lambda e, t=cc.n: e.wait_ge(cc.h, t))
    st_rb0 = st.n
    for j in range(16):
        stg = rowstage[:, (j % 4) * 1024:(j % 4 + 1) * 1024]
        if j >= 4:
            B.on("gpsimd", lambda e, t=st_rb0 + 16 * (j - 3): e.wait_ge(st.h, t))
        lx.n += 16
        B.on("gpsimd", lambda e, j=j, stg=stg: e.dma_start(
            out=stg, in_=rso_t.ap()[j * 128:(j + 1) * 128, :]).then_inc(lx.h, 16))
        B.on("gpsimd", lambda e, t=lx.n: e.wait_ge(lx.h, t))
        st.n += 16
        B.on("gpsimd", lambda e, j=j, stg=stg: e.dma_start(
            out=rsob_t.ap()[j * 128:(j + 1) * 128, :], in_=stg).then_inc(st.h, 16))
    B.on("sync", lambda e, t=st.n: e.wait_ge(st.h, t))
    for c in range(NCH):
        buf = c % 2
        B.on("gpsimd", lambda e, t=evv.n: e.wait_ge(evv.h, t))
        B.on("gpsimd", lambda e, t=st.n: e.wait_ge(st.h, t))
        for t4 in range(4):
            lx.n += 16
            B.on("gpsimd", lambda e, c=c, t4=t4, buf=buf: e.dma_start(
                out=xt[buf][:, t4 * 512:(t4 + 1) * 512],
                in_=x1_d[c, :, t4 * 512:(t4 + 1) * 512]).then_inc(lx.h, 16))
        aab = out_tile[c % 2][:, 0:2048]
        lt.n += 16
        B.on("sync", lambda e, t=AS.amo.n: e.wait_ge(AS.amo.h, t))
        if c >= 2:
            B.on("sync", lambda e, t=evv.n: e.wait_ge(evv.h, t))
        B.on("sync", lambda e, c=c, aab=aab: e.dma_start_transpose(
            out=aab, in_=rsob_t.ap()[:, c * 128:(c + 1) * 128]).then_inc(lt.h, 16))
        B.on("vector", lambda e, t=lt.n: e.wait_ge(lt.h, t))
        B.on("vector", lambda e, t=lx.n: e.wait_ge(lx.h, t))
        evv.n += 1

        def fadd(e, c=c, aab=aab, buf=buf):
            e.tensor_tensor(xt[buf][:, 0:2048], xt[buf][:, 0:2048], aab, ALU.add)
            e.tensor_scalar(xt[buf][:, 0:2048], xt[buf][:, 0:2048],
                            bout[:, c:c + 1], None, ALU.add).then_inc(evv.h, 1)
        B.on("vector", fadd)
        st.n += 16
        B.on("gpsimd", lambda e, t=evv.n: e.wait_ge(evv.h, t))
        B.on("gpsimd", lambda e, c=c, buf=buf: e.dma_start(
            out=x2p_d[c], in_=xt[buf][:, 0:2048]).then_inc(st.h, 16))

    finish_program(B, C)
    return nc, stack


def stage_B(ins, core, x1_fm):
    g = core % 2
    wqkv = np.asarray(ins["L1_attn_w_qkv"], np.float32)
    perm = np.concatenate([np.arange(g * 512, g * 512 + 512),
                           np.arange((1 - g) * 512, (1 - g) * 512 + 512)])
    wq = wqkv[:, 0:1024][:, perm]
    wk = wqkv[:, 1024:2048][:, perm]
    wv = wqkv[:, 2048:3072][:, perm]
    wfull = np.concatenate([wq, wk, wv], axis=1)
    wqkvr = np.stack([np.ascontiguousarray(
        wfull[:, nt * 512:(nt + 1) * 512].reshape(8, 128, 512))
        for nt in range(6)]).astype(bf)
    wo_own = np.asarray(ins["L1_attn_w_out"], np.float32)[g * 512:(g + 1) * 512, :]
    slopes = alibi_slopes()[g * 8:(g + 1) * 8]
    slp = np.zeros((1, 8), np.float32)
    slp[0, :] = slopes * 8.0
    im = dict(
        ident=np.eye(128, dtype=np.float32).astype(bf),
        x1=x1_fm.astype(np.float32),
        cdiag=causal_diag_masks().astype(bf),
        iota512=np.arange(512, dtype=np.float32).reshape(1, 512),
        wqkvr=wqkvr,
        wo=stage_w(wo_own),
        bout=stage_col(ins["L1_attn_b_out"]),
        slp=slp,
    )
    return im


def stage_B1_qkv(core, qkvb_pair):
    g = core % 2
    me, pr = qkvb_pair[g], qkvb_pair[1 - g]
    def pick(sec):
        mine = me[:, sec * 1024:sec * 1024 + 512]
        other = pr[:, sec * 1024 + 512:sec * 1024 + 1024]
        both = (np.concatenate([mine, other], 0) if g == 0
                else np.concatenate([other, mine], 0))
        return np.ascontiguousarray(both)
    return dict(qa=pick(0), ka=pick(1), va=pick(2))



NE_D = 2304


def build_D_attn(drc=True, dbg=False):
    stack = ExitStack()
    nc = bass.Bass("TRN2", target_bir_lowering=False, detect_race_conditions=drc)
    B = Builder(nc, stack)
    C = Ctx(B)
    AS = AttnSems(B)
    ap = lambda t: t.ap()
    NE = NE_D
    HS = 257 + S_OWN
    ident_d = ap(B.dram("ident", [128, 128], BF16, "ExternalInput"))
    yh_d = ap(B.dram("yh", [NCH, 128, HS], F32, "ExternalInput"))
    wqf_d = ap(B.dram("wqf", [8, 8, 128, 128], BF16, "ExternalInput"))
    wkf_d = ap(B.dram("wkf", [8, 8, 128, 128], BF16, "ExternalInput"))
    wvr_d = ap(B.dram("wvr", [8, 128, 1024], BF16, "ExternalInput"))
    wof_d = ap(B.dram("wof", [8, 8, 128, 128], BF16, "ExternalInput"))
    qs_d = ap(B.dram("qs", [128, 2], F32, "ExternalInput"))
    cs_d = ap(B.dram("cs", [128, 4 * NE], BF16, "ExternalInput"))
    lmask_d = ap(B.dram("lmask", [128, 4 * 512], BF16, "ExternalInput"))
    e2_d = ap(B.dram("e2", [128, 2], F32, "ExternalInput"))
    e2t_d = ap(B.dram("e2t", [2, 128], F32, "ExternalInput"))
    pm_d = ap(B.dram("pm", [128, 128], BF16, "ExternalInput"))
    yo_d = ap(B.dram("yo", [NCH, 128, S_OWN], F32, "ExternalOutput"))

    AF32 = Arena(B, "af32", 4700, F32)
    ABF = Arena(B, "abf", 77500, BF16)
    lx, mm, evv, evs, st, lw = C.lx, C.mm, C.evv, C.evs, C.st, C.lw
    C.init_consts(ident_d)
    sq0 = AF32.alloc(512)
    rows = AF32.alloc(1536)
    stat = AF32.alloc(16)
    qn1 = AF32.alloc(512)
    t1f = AF32.alloc(512)
    t2f = AF32.alloc(512)
    S1 = AF32.alloc(512)
    yrt = AF32.t[:, 0:4096]
    csb = B.sbuf("cs_sb", [128, 4 * NE], BF16)
    qscol = B.sbuf("qs_sb", [128, 2], F32)
    e2sb = B.sbuf("e2_sb", [128, 2], F32)
    e2t = B.sbuf("e2t_sb", [2, 128], F32)
    pmb = B.sbuf("pm_sb", [128, 128], BF16)
    lmask = B.sbuf("lm_sb", [128, 4 * 512], BF16)

    ybh = ABF.alloc(NCH * HS)
    outf = ybh[:, 0:NCH * S_OWN]
    Qr = ABF.alloc(NCH * S_OWN)
    Kr = ABF.alloc(NCH * NE)
    Vt = ABF.alloc(18 * 1024)
    P1 = ABF.alloc(512)
    PT1 = ABF.alloc(512)
    wbuf = ABF.alloc(2 * 8 * 128)
    wvsb = ABF.alloc(8 * 256)
    qnb = P1

    ybc = lambda c: ybh[:, c * HS:(c + 1) * HS]
    Qc = lambda c: Qr[:, c * S_OWN:(c + 1) * S_OWN]
    Kc = lambda c: Kr[:, c * NE:(c + 1) * NE]
    outc = lambda c: outf[:, c * S_OWN:(c + 1) * S_OWN]

    for nm, dst, srcd in (("cs", csb, cs_d), ("qs", qscol, qs_d),
                          ("e2", e2sb, e2_d), ("e2t", e2t, e2t_d),
                          ("pm", pmb, pm_d), ("lm", lmask, lmask_d)):
        lx.n += 16
        B.on("gpsimd", lambda e, dst=dst, srcd=srcd:
             e.dma_start(out=dst[:], in_=srcd).then_inc(lx.h, 16))
    for c in range(NCH):
        lx.n += 16
        B.on("gpsimd", lambda e, c=c: e.dma_start(
            out=ybc(c), in_=yh_d[c]).then_inc(lx.h, 16))
    ld0 = lx.n
    for eng in ("vector", "scalar", "tensor"):
        B.on(eng, lambda e, t=ld0: e.wait_ge(lx.h, t))

    def shs(c, lo, n):
        base = (1 + lo) if c < 4 else lo
        return ybc(c)[:, base:base + n]

    TIL = [(0, 512), (512, 512), (1024, 512), (1536, 512), (2048, 256)]
    QTIL = [(0, 512), (512, 512), (1024, 512), (1536, 512)]

    for lo, w in TIL:
        proj_fm(C, wkf_d, 8, 8, (lambda c, _t, lo=lo, w=w: shs(c, lo, w)), 1,
                (lambda kind, e, ps, m, _t, lo=lo, w=w: copy_any(
                    kind, e, Kc(m)[:, lo:lo + w], ps)), wbuf, tok=w)
    for lo, w in QTIL:
        proj_fm(C, wqf_d, 8, 8,
                (lambda c, _t, lo=lo, w=w: shs(c, 256 + lo, w)), 1,
                (lambda kind, e, ps, m, _t, lo=lo, w=w: copy_any(
                    kind, e, Qc(m)[:, lo:lo + w], ps)), wbuf, tok=w)

    def rotnorm(buffn, tblo, qk, tiles):
        ev0, es0 = evv.n, evs.n
        B.on("scalar", lambda e, t=ev0: e.wait_ge(evv.h, t))
        B.on("vector", lambda e, t=es0: e.wait_ge(evs.h, t))
        B.on("tensor", lambda e, t=ev0: e.wait_ge(evv.h, t))
        B.on("tensor", lambda e, t=es0: e.wait_ge(evs.h, t))
        for lo, w in tiles:
            for c in range(NCH):
                evs.n += 1
                B.on("scalar", lambda e, c=c, lo=lo, w=w: e.activation(
                    sq0[:, 0:w], buffn(c, lo, w), AF.Square).then_inc(evs.h, 1))
                B.on("tensor", lambda e, t=evs.n: e.wait_ge(evs.h, t))
                mm.n += 1
                B.on("tensor", lambda e, w=w: e.matmul(
                    C.psf[4][0:2, 0:w], e2sb[:], sq0[:, 0:w],
                    start=True, stop=True).then_inc(mm.h, 1))
                B.on("vector", lambda e, t=mm.n: e.wait_ge(mm.h, t))
                evv.n += 1
                B.on("vector", lambda e, w=w: e.tensor_copy(
                    rows[0:2, 0:w], C.psf[4][0:2, 0:w]).then_inc(evv.h, 1))
                B.on("scalar", lambda e, t=evv.n: e.wait_ge(evv.h, t))
                evs.n += 1
                B.on("scalar", lambda e, w=w: e.activation(
                    rows[0:2, 512:512 + w], rows[0:2, 0:w],
                    AF.Sqrt).then_inc(evs.h, 1))
                B.on("vector", lambda e, t=evs.n: e.wait_ge(evs.h, t))
                evv.n += 1

                def frec(e, w=w):
                    e.tensor_scalar(rows[0:2, 1024:1024 + w],
                                    rows[0:2, 512:512 + w],
                                    1e-12, None, ALU.max)
                    e.reciprocal(rows[0:2, 1024:1024 + w],
                                 rows[0:2, 1024:1024 + w]).then_inc(evv.h, 1)
                B.on("vector", frec)
                B.on("tensor", lambda e, t=evv.n: e.wait_ge(evv.h, t))
                mm.n += 1
                B.on("tensor", lambda e, w=w: e.matmul(
                    C.psf[5][:, 0:w], e2t[:], rows[0:2, 1024:1024 + w],
                    start=True, stop=True).then_inc(mm.h, 1))
                B.on("vector", lambda e, t=mm.n: e.wait_ge(mm.h, t))
                evv.n += 1

                def fqn(e, c=c, lo=lo, w=w):
                    e.scalar_tensor_tensor(qn1[:, 0:w], buffn(c, lo, w),
                                           qscol[:, qk:qk + 1], C.psf[5][:, 0:w],
                                           ALU.mult, ALU.mult)
                    e.tensor_copy(qnb[:, 0:w], qn1[:, 0:w]).then_inc(evv.h, 1)
                B.on("vector", fqn)
                B.on("tensor", lambda e, t=evv.n: e.wait_ge(evv.h, t))
                mm.n += 1
                B.on("tensor", lambda e, w=w: e.matmul(
                    C.psf[3][:, 0:w], pmb[:], qnb[:, 0:w],
                    start=True, stop=True).then_inc(mm.h, 1))
                B.on("vector", lambda e, t=mm.n: e.wait_ge(mm.h, t))
                evv.n += 1

                def frot(e, c=c, lo=lo, w=w):
                    co = tblo + lo
                    cst = csb[:, (2 * qk) * NE + co:(2 * qk) * NE + co + w]
                    snt = csb[:, (2 * qk + 1) * NE + co:(2 * qk + 1) * NE + co + w]
                    e.tensor_tensor(t1f[:, 0:w], qn1[:, 0:w], cst, ALU.mult)
                    e.tensor_tensor(t2f[:, 0:w], C.psf[3][:, 0:w], snt, ALU.mult)
                    e.tensor_tensor(buffn(c, lo, w), t1f[:, 0:w], t2f[:, 0:w],
                                    ALU.add).then_inc(evv.h, 1)
                B.on("vector", frot)

    rotnorm((lambda c, lo, w: Kc(c)[:, lo:lo + w]), 0, 1, TIL)
    rotnorm((lambda c, lo, w: Qc(c)[:, lo:lo + w]), 256, 0, QTIL)

    for qr in range(4):
        B.on("sync", lambda e, t=mm.n: e.wait_ge(mm.h, t))
        load_wsb(C, wvsb, wvr_d[:, :, qr * 256:(qr + 1) * 256], 8, 256)
        B.on("tensor", lambda e, t=lw.n: e.wait_ge(lw.h, t))
        proj_tok(C, wvsb, 8, 256,
                 (lambda c, j: shs(c, j * 128, 128)), 18,
                 (lambda kind, e, ps, j, qr=qr: copy_any(
                     kind, e, Vt[:, j * 1024 + qr * 256:
                                 j * 1024 + (qr + 1) * 256], ps)))
    mmq = mm.n
    if dbg:
        C.bar_all()
        dump_sb(B, C, "d_qr", Qr, [128, NCH * S_OWN], BF16)
        dump_sb(B, C, "d_kr", Kr, [128, NCH * NE], BF16)
        dump_sb(B, C, "d_vt", Vt, [128, 18 * 1024], BF16)
        C.bar_all()

    abufs = dict(S_sb=[S1, S1], P_bf=[P1, P1], PT_sb=[PT1, PT1], stat=stat,
                 serialize=True, pre=[("scalar", mm, mmq)])
    for qt in range(16):
        iters = []
        w0 = qt // 2
        mvar = (0 if qt < 2 else 2) + (qt % 2)
        for h in range(16):
            ch, base = h // 2, (h % 2) * 64
            iters.append(dict(
                q=Qc(ch)[base:base + 64, qt * 128:(qt + 1) * 128],
                wins=[(Kc(ch)[base:base + 64, w0 * 256:w0 * 256 + 512],
                       [("mask", lmask[:, mvar * 512:(mvar + 1) * 512])])],
                wlens=[512], scale=8.0,
                v=[Vt[:, (w0 * 2 + j) * 1024 + h * 64:
                      (w0 * 2 + j) * 1024 + h * 64 + 64] for j in range(4)],
                out=outc(ch)[base:base + 64, qt * 128:(qt + 1) * 128],
            ))
        attn_run(C, AS, iters, abufs)

    B.on("tensor", lambda e, t=AS.aeo.n: e.wait_ge(AS.aeo.h, t))
    B.on("gpsimd", lambda e, t=AS.aeo.n: e.wait_ge(AS.aeo.h, t))
    for tt in range(4):
        B.on("gpsimd", lambda e, t=evv.n: e.wait_ge(evv.h, t))
        B.on("gpsimd", lambda e, t=st.n: e.wait_ge(st.h, t))
        for c in range(NCH):
            lx.n += 16
            B.on("gpsimd", lambda e, c=c, tt=tt: e.dma_start(
                out=yrt[:, c * 512:(c + 1) * 512],
                in_=yh_d[c, :, 257 + tt * 512:257 + (tt + 1) * 512]
            ).then_inc(lx.h, 16))
        B.on("vector", lambda e, t=lx.n: e.wait_ge(lx.h, t))

        def wo_epi(kind, e, ps, m, _t):
            return e.tensor_tensor(yrt[:, m * 512:(m + 1) * 512],
                                   yrt[:, m * 512:(m + 1) * 512], ps, ALU.add)
        proj_fm(C, wof_d, 8, 8,
                (lambda c, _t, tt=tt: outc(c)[:, tt * 512:(tt + 1) * 512]), 1,
                wo_epi, wbuf, epi_eng=("vector",))
        for c in range(NCH):
            B.on("gpsimd", lambda e, t=evv.n - (NCH - 1 - c): e.wait_ge(evv.h, t))
            st.n += 16
            B.on("gpsimd", lambda e, c=c, tt=tt: e.dma_start(
                out=yo_d[c, :, tt * 512:(tt + 1) * 512],
                in_=yrt[:, c * 512:(c + 1) * 512]).then_inc(st.h, 16))
    finish_program(B, C)
    return nc, stack


def build_D_ffn(drc=True):
    stack = ExitStack()
    nc = bass.Bass("TRN2", target_bir_lowering=False, detect_race_conditions=drc)
    B = Builder(nc, stack)
    C = Ctx(B)
    ap = lambda t: t.ap()
    HS = 1 + S_OWN
    ident_d = ap(B.dram("ident", [128, 128], BF16, "ExternalInput"))
    yh_d = ap(B.dram("yh", [NCH, 128, HS], F32, "ExternalInput"))
    x2h_d = ap(B.dram("x2h", [NCH, 128, S_OWN], F32, "ExternalInput"))
    w1a_d = ap(B.dram("w1a", [22, 8, 128, 128], BF16, "ExternalInput"))
    w1g_d = ap(B.dram("w1g", [22, 8, 128, 128], BF16, "ExternalInput"))
    w2f_d = ap(B.dram("w2f", [8, 22, 128, 128], BF16, "ExternalInput"))
    cols_d = ap(B.dram("cols", [128, 80], F32, "ExternalInput"))
    yo_d = ap(B.dram("yo", [NCH, 128, S_OWN], F32, "ExternalOutput"))

    AF32 = Arena(B, "af32", 29800, F32)
    ABF = Arena(B, "abf", 23100, BF16)
    lx, mm, evv, evs, st, lw = C.lx, C.mm, C.evv, C.evs, C.st, C.lw
    C.init_consts(ident_d)
    sq0 = AF32.alloc(512)
    sq1 = AF32.alloc(512)
    bcbuf = AF32.alloc(1024)
    rows = AF32.alloc(2048)
    yt = AF32.alloc(4096)
    x2t = AF32.alloc(4096)
    gelt = AF32.alloc(512)
    ybh = AF32.alloc(NCH * HS)
    cols = B.sbuf("colsb", [128, 80], F32)
    xnb = ABF.alloc(NCH * 512)
    tb = ABF.alloc(22 * 512)
    wbuf = ABF.alloc(2 * 8 * 128)
    wbuf2 = ABF.alloc(2 * 22 * 128)

    ybc = lambda c: ybh[:, c * HS:(c + 1) * HS]

    DCOLS = {}
    off = 0
    for nm, n in [("lng", 8), ("lnb", 8), ("b1a", 22), ("b1g", 22), ("b2", 8)]:
        DCOLS[nm] = off
        off += n

    def col(nm, i=0):
        return cols[:, DCOLS[nm] + i:DCOLS[nm] + i + 1]

    lx.n += 16
    B.on("gpsimd", lambda e: e.dma_start(out=cols[:], in_=cols_d).then_inc(lx.h, 16))
    for c in range(NCH):
        lx.n += 16
        B.on("gpsimd", lambda e, c=c: e.dma_start(
            out=ybc(c), in_=yh_d[c]).then_inc(lx.h, 16))
    ld0 = lx.n
    for eng in ("vector", "scalar", "tensor"):
        B.on(eng, lambda e, t=ld0: e.wait_ge(lx.h, t))

    def shsf(c, lo, n):
        base = (1 + lo) if c < 4 else lo
        return ybc(c)[:, base:base + n]

    for tt in range(4):
        B.on("vector", lambda e, t=mm.n: e.wait_ge(mm.h, t))
        ln_fm(C, 1,
              (lambda c, _t, tt=tt: shsf(c, tt * 512, 512)),
              (lambda c, _t: xnb[:, c * 512:(c + 1) * 512]),
              1e-5, sq0, sq1, rows, bcbuf,
              gcol=lambda c: col("lng", c), bcol=lambda c: col("lnb", c))
        B.on("tensor", lambda e, t=evv.n: e.wait_ge(evv.h, t))
        mm0p = mm.n
        ep_hist = []
        for m in range(22):
            for wi_h, wd in ((0, w1a_d), (1, w1g_d)):
                wi = m * 2 + wi_h
                tgt = mm0p + (wi - 1 if wi >= 2 else 0)
                B.on("sync", lambda e, t=tgt: e.wait_ge(mm.h, t))
                b = (wi % 2) * 8 * 128
                lw.n += 16
                B.on("sync", lambda e, m=m, b=b, wd=wd: e.dma_start(
                    out=wbuf[:, b:b + 8 * 128].rearrange("p (c q) -> p c q", c=8),
                    in_=wd[m].rearrange("c p q -> p c q")).then_inc(lw.h, 16))
                B.on("tensor", lambda e, t=lw.n: e.wait_ge(lw.h, t))
                if wi >= 2 and ep_hist:
                    B.on("tensor", lambda e, t=ep_hist[-1]: e.wait_ge(evv.h, t))
                bank = C.psf[wi_h]
                mm.n += 1

                def fmmw(e, b=b, bank=bank):
                    inst = None
                    for c in range(NCH):
                        inst = e.matmul(bank[:, 0:512],
                                        wbuf[:, b + c * 128:b + (c + 1) * 128],
                                        xnb[:, c * 512:(c + 1) * 512],
                                        start=(c == 0), stop=(c == NCH - 1))
                    inst.then_inc(mm.h, 1)
                B.on("tensor", fmmw)
            B.on("scalar", lambda e, t=mm.n: e.wait_ge(mm.h, t))
            evs.n += 1
            B.on("scalar", lambda e, m=m: e.activation(
                gelt[:], C.psf[1][:, 0:512], AF.Gelu,
                bias=col("b1g", m)).then_inc(evs.h, 1))
            B.on("vector", lambda e, t=evs.n: e.wait_ge(evs.h, t))
            evv.n += 1
            ep_hist.append(evv.n)

            def fcomb(e, m=m):
                e.tensor_scalar(sq0[:], C.psf[0][:, 0:512], col("b1a", m),
                                None, ALU.add)
                e.tensor_tensor(tb[:, m * 512:(m + 1) * 512], sq0[:],
                                gelt[:], ALU.mult).then_inc(evv.h, 1)
            B.on("vector", fcomb)
        B.on("gpsimd", lambda e, t=st.n: e.wait_ge(st.h, t))
        for c in range(NCH):
            lx.n += 32
            B.on("gpsimd", lambda e, c=c, tt=tt: (
                e.dma_start(out=yt[:, c * 512:(c + 1) * 512],
                            in_=yh_d[c, :, 1 + tt * 512:1 + (tt + 1) * 512]
                            ).then_inc(lx.h, 16),
                e.dma_start(out=x2t[:, c * 512:(c + 1) * 512],
                            in_=x2h_d[c, :, tt * 512:(tt + 1) * 512]
                            ).then_inc(lx.h, 16)))
        B.on("vector", lambda e, t=lx.n: e.wait_ge(lx.h, t))

        def w2_epi(kind, e, ps, m, _t, tt=tt):
            e.scalar_tensor_tensor(yt[:, m * 512:(m + 1) * 512], ps,
                                   col("b2", m), yt[:, m * 512:(m + 1) * 512],
                                   ALU.add, ALU.add)
            return e.tensor_tensor(yt[:, m * 512:(m + 1) * 512],
                                   yt[:, m * 512:(m + 1) * 512],
                                   x2t[:, m * 512:(m + 1) * 512], ALU.add)
        proj_fm(C, w2f_d, 8, 22,
                (lambda c, _t: tb[:, c * 512:(c + 1) * 512]), 1,
                w2_epi, wbuf2, epi_eng=("vector",))
        for c in range(NCH):
            B.on("gpsimd", lambda e, t=evv.n - (NCH - 1 - c): e.wait_ge(evv.h, t))
            st.n += 16
            B.on("gpsimd", lambda e, c=c, tt=tt: e.dma_start(
                out=yo_d[c, :, tt * 512:(tt + 1) * 512],
                in_=yt[:, c * 512:(c + 1) * 512]).then_inc(st.h, 16))
    finish_program(B, C)
    return nc, stack



def _d_rotary_cs(core):
    g = core % 2
    NE = NE_D
    pos = np.maximum(g * S_OWN - 256 + np.arange(NE, dtype=np.float64), 0.0)
    half = np.arange(0, HD, 2, dtype=np.float64)
    inv_freq = 1.0 / 10000.0 ** (half / HD)
    fr = pos[:, None] * inv_freq[None, :]
    fr = np.concatenate([fr, fr], -1)
    base = (half + 0.4 * HD) / (1.4 * HD)
    power = (pos - S_FULL // 2) / 128.0
    sc = base[None, :] ** power[:, None]
    sc = np.concatenate([sc, sc], -1)
    cos, sin = np.cos(fr), np.sin(fr)
    tabs = [cos * sc, sin * sc, cos / sc, sin / sc]
    out = np.zeros((128, 4 * NE), np.float32)
    for t, tab in enumerate(tabs):
        out[:, t * NE:(t + 1) * NE] = np.tile(tab.T, (2, 1))
    return out.astype(bf)


def _d_lmask(core):
    g = core % 2
    r = np.arange(128)[:, None]
    kc = np.arange(512)[None, :]
    even = (r <= kc) & (kc <= r + 256)
    odd = (r + 128 <= kc) & (kc <= r + 384)
    clip = (kc >= 256) if g == 0 else np.ones_like(even)
    vs = [even & clip, odd & clip, even, odd]
    out = np.full((128, 4 * 512), NEG, np.float32)
    for v, ok in enumerate(vs):
        out[:, v * 512:(v + 1) * 512] = np.where(ok, 0.0, NEG)
    return out.astype(bf)


def _d_pm():
    Pb = np.zeros((64, 64), np.float32)
    Pb[:32, 32:] = -np.eye(32)
    Pb[32:, :32] = np.eye(32)
    P = np.zeros((128, 128), np.float32)
    P[:64, :64] = Pb
    P[64:, 64:] = Pb
    return np.ascontiguousarray(P.T).astype(bf)


def stage_D_attn(ins, core, li, y_full):
    b, g = core // 2, core % 2
    y = np.asarray(y_full[b], np.float32)
    own = y[g * S_OWN:(g + 1) * S_OWN]
    halo = (np.zeros((257, D), np.float32) if g == 0
            else y[S_OWN - 257:S_OWN])
    wqkv = np.asarray(ins["L1_ffn_Wqkv"][li], np.float32)
    qs = np.asarray(ins["L1_ffn_q_scale"][li], np.float32)
    ks = np.asarray(ins["L1_ffn_k_scale"][li], np.float32)
    qsc = np.stack([np.tile(qs, 2), np.tile(ks, 2)], 1)
    e2 = np.zeros((128, 2), np.float32)
    e2[:64, 0] = 1.0
    e2[64:, 1] = 1.0
    return dict(
        ident=np.eye(128, dtype=np.float32).astype(bf),
        yh=fm(np.concatenate([halo, own], 0)),
        wqf=stage_w(wqkv[:, 0:1024]),
        wkf=stage_w(wqkv[:, 1024:2048]),
        wvr=stage_w_rhs(wqkv[:, 2048:3072]),
        wof=stage_w(np.asarray(ins["L1_ffn_Wo"][li], np.float32)),
        qs=np.ascontiguousarray(qsc),
        cs=_d_rotary_cs(core),
        lmask=_d_lmask(core),
        e2=e2,
        e2t=np.ascontiguousarray(e2.T),
        pm=_d_pm(),
    )


def stage_D_ffn(ins, core, li, y_full, x2_fm=None):
    b, g = core // 2, core % 2
    y = np.asarray(y_full[b], np.float32)
    own = y[g * S_OWN:(g + 1) * S_OWN]
    halo = (np.zeros((1, D), np.float32) if g == 0
            else y[S_OWN - 1:S_OWN])
    W1 = np.asarray(ins["L1_ffn_W1"][li], np.float32)
    b1 = np.asarray(ins["L1_ffn_b1"][li], np.float32)
    DI = W1.shape[1] // 2
    cols = np.zeros((128, 80), np.float32)
    cols[:, 0:8] = stage_col(ins["L1_ffn_ln_g"][li])
    cols[:, 8:16] = stage_col(ins["L1_ffn_ln_b"][li])
    cols[:, 16:38] = np.pad(b1[:DI], (0, 86)).reshape(22, 128).T
    cols[:, 38:60] = np.pad(b1[DI:], (0, 86)).reshape(22, 128).T
    cols[:, 60:68] = stage_col(ins["L1_ffn_b2"][li])
    return dict(
        ident=np.eye(128, dtype=np.float32).astype(bf),
        yh=fm(np.concatenate([halo, own], 0)),
        x2h=(np.zeros((NCH, 128, S_OWN), np.float32) if x2_fm is None
             else np.asarray(x2_fm, np.float32)),
        w1a=stage_w(W1[:, :DI], pad_out=86),
        w1g=stage_w(W1[:, DI:], pad_out=86),
        w2f=stage_w(np.asarray(ins["L1_ffn_W2"][li], np.float32), pad_in=86),
        cols=cols,
    )


def unfm(arr):
    a = np.asarray(arr, np.float32)
    return np.ascontiguousarray(a.transpose(2, 0, 1).reshape(a.shape[2], D))



_PROG_CACHE = {}


def _prog(name, builder):
    if name not in _PROG_CACHE:
        _PROG_CACHE[name] = builder()
    return _PROG_CACHE[name][0]


def kernel(**inputs):
    ins = inputs
    cores = list(range(NCORES))

    nc_a = _prog("A", lambda: build_A(drc=False))
    ra = run_bass_kernel_spmd(nc_a, [stage_A(ins, c) for c in cores],
                              core_ids=cores)
    x1 = [np.asarray(ra.results[c]["x1o"], np.float32) for c in cores]

    nc_b0 = _prog("B0", lambda: build_B(stop="b0", drc=False))
    zq = np.zeros((S_FULL, 512), bf)
    im_b = [stage_B(ins, c, x1[c]) for c in cores]
    for c in cores:
        im_b[c].update(qa=zq, ka=zq, va=zq)
    rb0 = run_bass_kernel_spmd(nc_b0, im_b, core_ids=cores)
    qkvb = [np.asarray(rb0.results[c]["qkvb"]) for c in cores]
    xn1 = [np.asarray(rb0.results[c]["xn1o"], np.float32) for c in cores]

    nc_b1 = _prog("B1", lambda: build_B(stop="all", drc=False))
    im_b1 = [stage_B(ins, c, x1[c]) for c in cores]
    for c in cores:
        pair = [qkvb[(c // 2) * 2], qkvb[(c // 2) * 2 + 1]]
        im_b1[c].update(stage_B1_qkv(c, pair))
    rb1 = run_bass_kernel_spmd(nc_b1, im_b1, core_ids=cores)
    x2p = [np.asarray(rb1.results[c]["x2p"], np.float32) for c in cores]

    def assemble(per_core):
        y = np.zeros((4, S_FULL, D), np.float32)
        for c in cores:
            b, g = c // 2, c % 2
            y[b, g * S_OWN:(g + 1) * S_OWN] = unfm(per_core[c])
        return y

    y_full = assemble(xn1)
    nc_da = _prog("Dattn", lambda: build_D_attn(drc=False))
    nc_df = _prog("Dffn", lambda: build_D_ffn(drc=False))
    for li in range(6):
        rda = run_bass_kernel_spmd(
            nc_da, [stage_D_attn(ins, c, li, y_full) for c in cores],
            core_ids=cores)
        y_full = assemble([rda.results[c]["yo"] for c in cores])
        rdf = run_bass_kernel_spmd(
            nc_df, [stage_D_ffn(ins, c, li, y_full,
                                x2_fm=(x2p[c] if li == 5 else None))
                    for c in cores],
            core_ids=cores)
        y_full = assemble([rdf.results[c]["yo"] for c in cores])

    return np.ascontiguousarray(y_full)
